# version 03
# baseline (speedup 1.0000x reference)
import numpy as np
import ml_dtypes
from contextlib import ExitStack

import concourse.bass as bass
import concourse.mybir as mybir
from concourse.bass_utils import run_bass_kernel_spmd

F32 = mybir.dt.float32
BF16 = mybir.dt.bfloat16
AF = mybir.ActivationFunctionType
ALU = mybir.AluOpType
AX = mybir.AxisListType

NCORE = 8
T = 2048
D = 2048
HALO = 3
DC = 16
INNER = 4096
CCN = 32
E = 8
NT = 16
NQ = 4
TPAD = 2176
EPS = 1e-5
ENGS = ("sync", "gpsimd", "tensor", "scalar", "vector")

_BUILD_CACHE = {}


def build(dev=False, stages="abc", ln_tiles=17, dump_xn=True, ln_start=-1):
    nc = bass.Bass("TRN2", target_bir_lowering=False, debug=False)

    def din(name, shape, dtype=F32):
        return nc.dram_tensor(name, shape, dtype, kind="ExternalInput").ap()

    x = din("x", [T, D])
    xh = din("xh", [HALO, D])
    gmask = din("gmask", [128, 4])
    gm1m = din("gm1m", [128, 4])
    winT = din("winT", [D, INNER])
    wgT = din("wgT", [D, INNER])
    woutT = din("woutT", [INNER, D])
    convw = din("convw", [128, 128])
    convb = din("convb", [128, CCN])
    avecl = din("avecl", [128, CCN])
    bvec = din("bvec", [128, CCN])
    cvec = din("cvec", [128, CCN])
    dvec = din("dvec", [128, CCN])
    dtw = din("dtw", [128, 2 * CCN])
    dtb = din("dtb", [2, CCN])
    sel2 = din("sel2", [2, 128])
    actc = din("actc", [128, 8])
    sel8 = din("sel8", [8, 8 * 128])
    ones1 = din("ones1", [1, 128])
    identf = din("identf", [128, 128])
    identb = din("identb", [128, 128], BF16)
    rwT = din("rwT", [128, DC * E])
    wg = din("wg", [E, D, 1024])
    wu = din("wu", [E, D, 1024])
    wd = din("wd", [E, 1024, D])
    sgw = din("sgw", [D, 1024])
    suw = din("suw", [D, 1024])
    sdw = din("sdw", [1024, D])
    sgb = din("sgb", [128, 8])
    sub = din("sub", [128, 8])
    sdb = din("sdb", [1, D])

    out = nc.dram_tensor("out", [T, D], F32, kind="ExternalOutput").ap()

    combined_sc = nc.dram_tensor("combined_sc", [CCN, 128, T], BF16, kind="Internal").ap()
    hw_sc = nc.dram_tensor("hw_sc", [E + 1, 8, 128, T], BF16, kind="Internal").ap()
    ccin = nc.dram_tensor("ccin", [CCN, 2, 128], F32, kind="Internal").ap()
    ccout = nc.dram_tensor("ccout", [CCN, 8, 128], F32, kind="Internal").ap()

    if dev:
        dev_xnfm = nc.dram_tensor("dev_xnfm", [128, DC, TPAD], BF16, kind="ExternalOutput").ap()
        dev_comb = nc.dram_tensor("dev_comb", [CCN, 128, T], BF16, kind="ExternalOutput").ap()
        dev_hw = nc.dram_tensor("dev_hw", [E + 1, 8, 128, T], BF16, kind="ExternalOutput").ap()
        dev_combfm = nc.dram_tensor("dev_combfm", [8, T], F32, kind="ExternalOutput").ap()
        dev_lg = nc.dram_tensor("dev_lg", [8, T], F32, kind="ExternalOutput").ap()
        dev_lgtm = nc.dram_tensor("dev_lgtm", [128, NT * 8], F32, kind="ExternalOutput").ap()
        dev_ctm = nc.dram_tensor("dev_ctm", [128, NT * 8], F32, kind="ExternalOutput").ap()
        dev_m12 = nc.dram_tensor("dev_m12", [128, 2 * NT], F32, kind="ExternalOutput").ap()

    plan = {e: [] for e in ENGS}
    cnt = {}

    STAGE = {"on": True}

    def inc(name, v=1):
        if not STAGE["on"]:
            return cnt.get(name, 0)
        cnt[name] = cnt.get(name, 0) + v
        return cnt[name]

    def val(name):
        return cnt.get(name, 0)

    sems = {}

    def on(eng, *waits):
        if not STAGE["on"]:
            return lambda fn: None
        w = []
        for (s, v) in waits:
            if v > 0:
                assert v <= cnt.get(s, 0), f"wait {s}>={v} references future inc (have {cnt.get(s,0)})"
                w.append((s, v))

        def deco(fn):
            plan[eng].append((w, fn))
            return None
        return deco

    NONE = ("ldc", 0)

    with ExitStack() as stack:
        def sb(name, shape, dtype):
            return stack.enter_context(nc.sbuf_tensor(name, shape, dtype))

        def psum(name, shape, dtype):
            return stack.enter_context(nc.psum_tensor(name, shape, dtype))

        sem_names = (["ldc", "gpc", "ldx0", "ldx1", "wa0", "wa1", "wb0", "wb1",
                      "wc", "ca0", "ca1", "xr0", "xr1", "sc", "sum", "outd", "trx",
                      "pe", "act", "dve"] + [f"cc{k}" for k in range(CCN)])
        for n in sem_names:
            sems[n] = stack.enter_context(nc.semaphore(n))

        c_gmask = sb("c_gmask", [128, 4], F32)
        c_gm1m = sb("c_gm1m", [128, 4], F32)
        c_convw = sb("c_convw", [128, 128], F32)
        c_convb = sb("c_convb", [128, CCN], F32)
        c_avecl = sb("c_avecl", [128, CCN], F32)
        c_aneg = sb("c_aneg", [128, CCN], F32)
        c_bvec = sb("c_bvec", [128, CCN], F32)
        c_cvec = sb("c_cvec", [128, CCN], F32)
        c_dvec = sb("c_dvec", [128, CCN], F32)
        c_dtw = sb("c_dtw", [128, 2 * CCN], F32)
        c_dtb = sb("c_dtb", [2, CCN], F32)
        c_sel2 = sb("c_sel2", [2, 128], F32)
        c_actc = sb("c_actc", [128, 8], F32)
        c_sel8 = sb("c_sel8", [8, 8 * 128], F32)
        c_ones1 = sb("c_ones1", [1, 128], F32)
        c_identf = sb("c_identf", [128, 128], F32)
        c_identb = sb("c_identb", [128, 128], BF16)
        c_sgb = sb("c_sgb", [128, 8], F32)
        c_sub = sb("c_sub", [128, 8], F32)
        c_sdb = sb("c_sdb", [1, D], F32)

        psA = [psum(f"psA{i}", [128, 512], F32) for i in range(4)]
        psB = psum("psB", [128, 512], F32)
        psD = psum("psD", [128, 512], F32)
        psE = psum("psE", [128, 512], F32)

        @on("sync")
        def _(eng):
            for dst, src in ((c_gmask, gmask), (c_gm1m, gm1m), (c_convw, convw),
                             (c_convb, convb), (c_avecl, avecl), (c_bvec, bvec),
                             (c_cvec, cvec), (c_dvec, dvec), (c_dtw, dtw),
                             (c_dtb, dtb), (c_sel2, sel2), (c_sel8, sel8), (c_actc, actc), (c_ones1, ones1),
                             (c_identf, identf), (c_identb, identb),
                             (c_sgb, sgb), (c_sub, sub), (c_sdb, sdb)):
                eng.dma_start(out=dst[:], in_=src).then_inc(sems["ldc"], 16)
        inc("ldc", 16 * 19)
        NCONST = val("ldc")

        @on("scalar", ("ldc", NCONST))
        def _(eng):
            eng.activation(out=c_aneg[:], in_=c_avecl[:], func=AF.Exp).then_inc(sems["act"], 1)
        inc("act")

        @on("vector", ("act", val("act")))
        def _(eng):
            eng.tensor_scalar_mul(c_aneg[:], c_aneg[:], -1.0).then_inc(sems["dve"], 1)
        ANEG_DVE = inc("dve")

        stABx = stack.enter_context(ExitStack())
        xn_fm = stABx.enter_context(nc.sbuf_tensor("xn_fm", [128, DC, TPAD], BF16))

        with ExitStack() as stA:
            stLN = ExitStack()

            def sbl(name, shape, dtype):
                return stLN.enter_context(nc.sbuf_tensor(name, shape, dtype))

            xtile = [sbl(f"xtile{p}", [128, D], F32) for p in range(2)]
            xsq = sbl("xsq", [128, D], F32)
            xn_tm = [sbl(f"xn_tm{p}", [128, D], BF16) for p in range(2)]
            nsum = sbl("nsum", [128, 1], F32)
            sqsum = sbl("sqsum", [128, 1], F32)
            mneg = sbl("mneg", [128, 1], F32)
            ex2 = sbl("ex2", [128, 1], F32)
            negvar = sbl("negvar", [128, 1], F32)
            lnv = sbl("lnv", [128, 1], F32)
            rstd = sbl("rstd", [128, 1], F32)
            lnb = sbl("lnb", [128, 1], F32)

            st_ln = {"xtile": [NONE, NONE], "xntm": [NONE, NONE], "psT": NONE}

            def emit_ln_tile(i):
                p = (i + 1) % 2
                if i == -1:
                    @on("vector")
                    def _(eng, p=p):
                        eng.memset(xtile[p][:], 0.0).then_inc(sems["dve"], 1)
                    ms_done = inc("dve")

                    @on("sync", ("dve", ms_done))
                    def _(eng, p=p):
                        eng.dma_start(out=xtile[p][0:HALO, :], in_=xh).then_inc(sems[f"ldx{p}"], 16)
                else:
                    @on("sync", st_ln["xtile"][p])
                    def _(eng, p=p, i=i):
                        eng.dma_start(out=xtile[p][:], in_=x[i * 128:(i + 1) * 128, :]
                                      ).then_inc(sems[f"ldx{p}"], 16)
                ldv = inc(f"ldx{p}", 16)

                @on("vector", (f"ldx{p}", ldv))
                def _(eng, p=p):
                    eng.reduce_sum(out=nsum[:], in_=xtile[p][:], axis=AX.X, negate=True).then_inc(sems["dve"], 1)
                inc("dve")

                @on("scalar", (f"ldx{p}", ldv))
                def _(eng, p=p):
                    eng.activation(out=xsq[:], in_=xtile[p][:], func=AF.Square,
                                   accum_out=sqsum[:]).then_inc(sems["act"], 1)
                sq_done = inc("act")

                @on("vector", ("act", sq_done))
                def _(eng):
                    eng.tensor_scalar_mul(mneg[:], nsum[:], 1.0 / D)
                    eng.tensor_scalar_mul(ex2[:], sqsum[:], 1.0 / D)
                    eng.tensor_scalar(out=negvar[:], in0=mneg[:], scalar1=mneg[:],
                                      scalar2=ex2[:], op0=ALU.mult, op1=ALU.subtract).then_inc(sems["dve"], 1)
                stats_done = inc("dve")

                @on("scalar", ("dve", stats_done))
                def _(eng):
                    eng.activation(out=lnv[:], in_=negvar[:], func=AF.Ln,
                                   scale=c_actc[:, 2:3], bias=c_actc[:, 0:1])
                    eng.activation(out=rstd[:], in_=lnv[:], func=AF.Exp,
                                   scale=c_actc[:, 1:2]).then_inc(sems["act"], 1)
                rstd_done = inc("act")

                @on("vector", ("act", rstd_done))
                def _(eng):
                    eng.tensor_mul(lnb[:], mneg[:], rstd[:]).then_inc(sems["dve"], 1)
                lnb_done = inc("dve")

                @on("scalar", ("dve", lnb_done), st_ln["xntm"][p])
                def _(eng, p=p):
                    eng.activation(out=xn_tm[p][:], in_=xtile[p][:], func=AF.Identity,
                                   scale=rstd[:], bias=lnb[:]).then_inc(sems["act"], 1)
                xn_done = inc("act")
                st_ln["xtile"][p] = ("act", xn_done)

                if i == -1:
                    @on("sync", ("act", xn_done))
                    def _(eng, p=p):
                        eng.dma_start(out=xn_tm[1 - p][:].rearrange("q (dc t) -> q dc t", t=128),
                                      in_=xn_tm[p][:], transpose=True).then_inc(sems["trx"], 16)
                    trv = inc("trx", 16)

                    @on("vector", ("trx", trv))
                    def _(eng, p=p):
                        eng.tensor_copy(
                            out=xn_fm[:, :, T:T + HALO],
                            in_=xn_tm[1 - p][:].rearrange("q (dc t) -> q dc t", t=128)[:, :, 0:HALO]
                        ).then_inc(sems["dve"], 1)
                    halo_cp = inc("dve")
                    st_ln["xntm"][p] = ("dve", halo_cp)
                    st_ln["xntm"][1 - p] = ("dve", halo_cp)
                else:
                    @on("sync", ("act", xn_done), ("dve", val("dve")))
                    def _(eng, p=p, i=i):
                        off = i * 128
                        eng.dma_start(out=xn_fm[:, :, off:off + 128],
                                      in_=xn_tm[p][:], transpose=True).then_inc(sems["trx"], 16)
                    st_ln["xntm"][p] = ("trx", inc("trx", 16))

            for i in range(ln_start, min(NT, ln_start + ln_tiles)):
                emit_ln_tile(i)
            LN_DVE = val("dve")
            LN_ACT = val("act")
            LN_TRX = val("trx")
            if dev and dump_xn:
                @on("sync", ("dve", LN_DVE), ("trx", LN_TRX))
                def _(eng):
                    eng.dma_start(out=dev_xnfm, in_=xn_fm[:]).then_inc(sems["outd"], 16)
                inc("outd", 16)
            stLN.close()

            STAGE["on"] = "a" in stages

            def sba(name, shape, dtype):
                return stA.enter_context(nc.sbuf_tensor(name, shape, dtype))

            wiT_sb = [sba(f"wiT{p}", [128, DC, 128], BF16) for p in range(2)]
            wgT_sb = [sba(f"wgTs{p}", [128, DC, 128], BF16) for p in range(2)]
            proj = sba("proj", [128, T + HALO], F32)
            acc1 = sba("acc1", [128, T], F32)
            acc2 = sba("acc2", [128, T], F32)
            Xc = sba("Xc", [128, T], F32)
            gatec = sba("gatec", [128, T], F32)
            a_sb = sba("a_sb", [128, T], F32)
            u_sb = sba("u_sb", [128, T], F32)
            s_sb = sba("s_sb", [128, T], F32)
            comb_bf = sba("comb_bf", [128, T], BF16)
            dtraw_t = sba("dtraw", [2, T], F32)
            dtraw = dtraw_t[:]
            dtt1_t = sba("dtt1", [2, T], F32)
            dtt1 = dtt1_t[:]
            dtt2_t = sba("dtt2", [2, T], F32)
            dtt2 = dtt2_t[:]
            dt_sb = sba("dt_sb", [2, T], F32)
            summ = sba("summ", [128, 2], F32)
            gath = sba("gath", [8, 128], F32)
            gathT = sba("gathT", [128, 8], F32)
            sini = [sba(f"sini{p}", [128, 1], F32) for p in range(2)]
            tmpa = sba("tmpa", [128, 1], F32)
            tmpse = sba("tmpse", [128, 1], F32)

            stA_st = {"wa": [NONE, NONE], "psA": NONE, "psB": NONE, "psD": NONE,
                      "psE": NONE, "prev_comb": 0, "prev_sc": 0, "prev_pub": 0,
                      "prev_scan1": 0, "gath_ld": 0}

            def emit_tail(cc):
                @on("tensor", ("sum", stA_st["gath_ld"]), stA_st["psB"])
                def _(eng):
                    eng.matmul(psB[:, 0:8], gath[0:8, :], c_identf[0:8, 0:8],
                               start=True, stop=True).then_inc(sems["pe"], 1)
                gathT_mm = inc("pe")

                @on("vector", ("pe", gathT_mm))
                def _(eng):
                    eng.tensor_copy(out=gathT[:], in_=psB[:, 0:8]).then_inc(sems["dve"], 1)
                stA_st["psB"] = ("dve", inc("dve"))

                @on("vector", ("ldc", NCONST), ("sc", stA_st["prev_sc"]))
                def _(eng, cc=cc):
                    eng.memset(sini[0][:], 0.0)
                    cur, nxt = 0, 1
                    for r in range(3):
                        eng.tensor_scalar(out=tmpa[:], in0=gathT[:, 2 * r:2 * r + 1],
                                          scalar1=c_gmask[:, r:r + 1], scalar2=c_gm1m[:, r:r + 1],
                                          op0=ALU.mult, op1=ALU.add)
                        eng.tensor_scalar(out=tmpse[:], in0=gathT[:, 2 * r + 1:2 * r + 2],
                                          scalar1=c_gmask[:, r:r + 1], scalar2=None, op0=ALU.mult)
                        eng.scalar_tensor_tensor(out=sini[nxt][:], in0=tmpa[:], scalar=sini[cur][:],
                                                 in1=tmpse[:], op0=ALU.mult, op1=ALU.add)
                        cur, nxt = nxt, cur
                    eng.tensor_tensor_scan(out=s_sb[:], data0=a_sb[:], data1=u_sb[:],
                                           initial=sini[cur][:], op0=ALU.mult, op1=ALU.add)
                    eng.tensor_scalar(out=acc1[:], in0=Xc[:], scalar1=c_dvec[:, cc:cc + 1],
                                      scalar2=None, op0=ALU.mult)
                    eng.scalar_tensor_tensor(out=a_sb[:], in0=s_sb[:], scalar=c_cvec[:, cc:cc + 1],
                                             in1=acc1[:], op0=ALU.mult, op1=ALU.add)
                    eng.tensor_mul(out=comb_bf[:], in0=a_sb[:],
                                   in1=gatec[:]).then_inc(sems["dve"], 1)
                combined_done = inc("dve")
                stA_st["prev_comb"] = combined_done

                @on("scalar", ("dve", combined_done))
                def _(eng, cc=cc):
                    eng.dma_start(out=combined_sc[cc], in_=comb_bf[:]).then_inc(sems["sc"], 16)
                stA_st["prev_sc"] = inc("sc", 16)

            def emit_wA(cc, extra=()):
                wp = cc % 2

                @on("gpsimd", stA_st["wa"][wp], *extra)
                def _(eng, cc=cc, wp=wp):
                    wi3 = winT.rearrange("(d p) c -> p d c", p=128)
                    wg3 = wgT.rearrange("(d p) c -> p d c", p=128)
                    for h in range(2):
                        eng.dma_start(out=wiT_sb[wp][:, h * 8:(h + 1) * 8, :],
                                      in_=wi3[:, h * 8:(h + 1) * 8, cc * 128:(cc + 1) * 128]
                                      ).then_inc(sems[f"wa{wp}"], 16)
                        eng.dma_start(out=wgT_sb[wp][:, h * 8:(h + 1) * 8, :],
                                      in_=wg3[:, h * 8:(h + 1) * 8, cc * 128:(cc + 1) * 128]
                                      ).then_inc(sems[f"wa{wp}"], 16)
                inc(f"wa{wp}", 64)

            emit_wA(0, extra=(("dve", LN_DVE), ("act", LN_ACT), ("trx", LN_TRX)))
            emit_wA(1)

            for cc in range(CCN):
                wp = cc % 2
                @on("tensor", (f"wa{wp}", 64 * (cc // 2 + 1)), ("dve", LN_DVE),
                    ("trx", LN_TRX), stA_st["psA"], stA_st["psD"])
                def _(eng, cc=cc, wp=wp):
                    for d in range(DC):
                        st = (d == 0)
                        sp = (d == DC - 1)
                        lhs = wiT_sb[wp][:, d, :]
                        for q in range(NQ):
                            eng.matmul(psA[q][:], lhs,
                                       xn_fm[:, d, q * 512:(q + 1) * 512],
                                       start=st, stop=sp)
                        mm = eng.matmul(psD[:, 0:HALO], lhs, xn_fm[:, d, T:T + HALO], start=st, stop=sp)
                    mm.then_inc(sems["pe"], 1)
                inproj_done = inc("pe")

                @on("scalar", ("pe", inproj_done), ("dve", stA_st["prev_scan1"]))
                def _(eng):
                    eng.copy(out=proj[:, 0:HALO], in_=psD[:, 0:HALO])
                    for q in range(NQ - 1):
                        eng.copy(out=proj[:, HALO + q * 512:HALO + (q + 1) * 512], in_=psA[q][:])
                    eng.copy(out=proj[:, HALO + 3 * 512:HALO + 4 * 512],
                             in_=psA[3][:]).then_inc(sems["act"], 1)
                proj_done = inc("act")
                stA_st["psD"] = ("act", proj_done)

                @on("tensor", ("act", proj_done))
                def _(eng, cc=cc, wp=wp):
                    for d in range(DC):
                        st = (d == 0)
                        sp = (d == DC - 1)
                        lhs = wgT_sb[wp][:, d, :]
                        for q in range(NQ):
                            mm = eng.matmul(psA[q][:], lhs,
                                            xn_fm[:, d, q * 512:(q + 1) * 512],
                                            start=st, stop=sp)
                    mm.then_inc(sems["pe"], 1)
                gatemm_done = inc("pe")

                if cc + 2 < CCN:
                    stA_st["wa"][wp] = ("pe", gatemm_done)
                    emit_wA(cc + 2)

                @on("vector", ("act", proj_done))
                def _(eng, cc=cc):
                    eng.tensor_scalar(out=acc1[:], in0=proj[:, 0:T],
                                      scalar1=c_convw[:, cc * 4:cc * 4 + 1],
                                      scalar2=c_convb[:, cc:cc + 1], op0=ALU.mult, op1=ALU.add)
                    eng.scalar_tensor_tensor(out=acc2[:], in0=proj[:, 1:T + 1],
                                             scalar=c_convw[:, cc * 4 + 1:cc * 4 + 2], in1=acc1[:],
                                             op0=ALU.mult, op1=ALU.add)
                    eng.scalar_tensor_tensor(out=acc1[:], in0=proj[:, 2:T + 2],
                                             scalar=c_convw[:, cc * 4 + 2:cc * 4 + 3], in1=acc2[:],
                                             op0=ALU.mult, op1=ALU.add)
                    eng.scalar_tensor_tensor(out=acc2[:], in0=proj[:, 3:T + 3],
                                             scalar=c_convw[:, cc * 4 + 3:cc * 4 + 4], in1=acc1[:],
                                             op0=ALU.mult, op1=ALU.add).then_inc(sems["dve"], 1)
                conv_done = inc("dve")

                if cc >= 1:
                    emit_tail(cc - 1)

                @on("scalar", ("pe", gatemm_done), ("dve", stA_st["prev_comb"]))
                def _(eng):
                    for q in range(NQ):
                        a = eng.activation(out=gatec[:, q * 512:(q + 1) * 512], in_=psA[q][:],
                                           func=AF.Sigmoid)
                    a.then_inc(sems["act"], 1)
                gate_done = inc("act")
                stA_st["psA"] = ("act", gate_done)

                @on("scalar", ("dve", conv_done), ("dve", stA_st["prev_comb"]))
                def _(eng):
                    eng.activation(out=Xc[:], in_=acc2[:], func=AF.Silu).then_inc(sems["act"], 1)
                X_done = inc("act")

                for q in range(NQ):
                    bank, key = (psD, "psD") if q % 2 == 0 else (psE, "psE")

                    @on("tensor", ("act", X_done), stA_st[key])
                    def _(eng, cc=cc, q=q, bank=bank):
                        eng.matmul(bank[0:2, :], c_dtw[:, 2 * cc:2 * cc + 2],
                                   Xc[:, q * 512:(q + 1) * 512], start=True,
                                   stop=True).then_inc(sems["pe"], 1)
                    mm_done = inc("pe")

                    @on("scalar", ("pe", mm_done))
                    def _(eng, q=q, bank=bank, cc=cc):
                        eng.activation(out=dtraw_t[:, q * 512:(q + 1) * 512], in_=bank[0:2, :],
                                       func=AF.Identity, bias=c_dtb[:, cc:cc + 1]).then_inc(sems["act"], 1)
                    stA_st[key] = ("act", inc("act"))

                @on("scalar", ("act", val("act")))
                def _(eng):
                    eng.activation(out=dt_sb[:], in_=dtraw, func=AF.Relu)
                    eng.activation(out=dtt1[:], in_=dtraw, func=AF.Abs).then_inc(sems["act"], 1)
                inc("act")

                @on("scalar", ("act", val("act")))
                def _(eng):
                    eng.activation(out=dtt2, in_=dtt1, func=AF.Exp,
                                   scale=c_actc[0:2, 2:3]).then_inc(sems["act"], 1)
                inc("act")

                @on("scalar", ("act", val("act")))
                def _(eng):
                    eng.activation(out=dtt1, in_=dtt2, func=AF.Ln,
                                   bias=c_actc[0:2, 3:4]).then_inc(sems["act"], 1)
                ln1p_done = inc("act")

                @on("vector", ("act", ln1p_done))
                def _(eng):
                    eng.tensor_add(out=dt_sb[:], in0=dt_sb[:], in1=dtt1).then_inc(sems["dve"], 1)
                dt_done = inc("dve")

                for q in range(NQ):
                    bank, key = (psB, "psB") if q % 2 == 0 else (psE, "psE")

                    @on("tensor", ("dve", dt_done), stA_st[key])
                    def _(eng, q=q, bank=bank):
                        eng.matmul(bank[:], c_sel2[:], dt_sb[:, q * 512:(q + 1) * 512],
                                   start=True, stop=True).then_inc(sems["pe"], 1)
                    bc_done = inc("pe")

                    @on("scalar", ("pe", bc_done), ("dve", ANEG_DVE))
                    def _(eng, q=q, bank=bank, cc=cc):
                        eng.activation(out=a_sb[:, q * 512:(q + 1) * 512], in_=bank[:],
                                       func=AF.Exp, scale=c_aneg[:, cc:cc + 1]).then_inc(sems["act"], 1)
                    aq_done = inc("act")

                    @on("vector", ("act", aq_done))
                    def _(eng, q=q, bank=bank, cc=cc):
                        eng.scalar_tensor_tensor(out=u_sb[:, q * 512:(q + 1) * 512], in0=bank[:],
                                                 scalar=c_bvec[:, cc:cc + 1],
                                                 in1=Xc[:, q * 512:(q + 1) * 512],
                                                 op0=ALU.mult, op1=ALU.mult).then_inc(sems["dve"], 1)
                    stA_st[key] = ("dve", inc("dve"))

                @on("vector", ("sum", stA_st["prev_pub"]))
                def _(eng):
                    eng.tensor_tensor_scan(out=s_sb[:], data0=a_sb[:], data1=u_sb[:],
                                           initial=0.0, op0=ALU.mult, op1=ALU.add)
                    eng.tensor_tensor_scan(out=proj[:, 0:T], data0=a_sb[:], data1=a_sb[:],
                                           initial=1.0, op0=ALU.mult, op1=ALU.bypass)
                    eng.tensor_copy(out=summ[:, 0:1], in_=proj[:, T - 1:T])
                    eng.tensor_copy(out=summ[:, 1:2],
                                    in_=s_sb[:, T - 1:T]).then_inc(sems["dve"], 1)
                scan1_done = inc("dve")
                stA_st["prev_scan1"] = scan1_done

                @on("gpsimd", ("dve", scan1_done))
                def _(eng, cc=cc):
                    eng.dma_start(out=ccin[cc, 0:1, :].rearrange("a c -> c a"),
                                  in_=summ[:, 0:1]).then_inc(sems["sum"], 16)
                    eng.dma_start(out=ccin[cc, 1:2, :].rearrange("a c -> c a"),
                                  in_=summ[:, 1:2]).then_inc(sems["sum"], 16)
                pub_done = inc("sum", 32)
                stA_st["prev_pub"] = pub_done

                @on("gpsimd", ("sum", pub_done))
                def _(eng, cc=cc):
                    eng.collective_compute(
                        "AllGather", ALU.bypass,
                        replica_groups=[[0, 1, 2, 3], [4, 5, 6, 7]],
                        ins=[ccin[cc]], outs=[ccout[cc]],
                    ).then_inc(sems[f"cc{cc}"])
                inc(f"cc{cc}")

                @on("gpsimd", (f"cc{cc}", 1))
                def _(eng, cc=cc):
                    eng.dma_start(out=gath[:], in_=ccout[cc]).then_inc(sems["sum"], 16)
                stA_st["gath_ld"] = inc("sum", 16)

            if CCN >= 1 and "a" in stages:
                emit_tail(CCN - 1)

            STAGE["on"] = True
            A_END = dict(pe=val("pe"), act=val("act"), dve=val("dve"), sc=val("sc"))

        with ExitStack() as stB:
            STAGE["on"] = "b" in stages

            def sbb(name, shape, dtype):
                return stB.enter_context(nc.sbuf_tensor(name, shape, dtype))

            comb_fm = sbb("comb_fm", [8, T], F32)
            rwT_sb = sbb("rwT_sb", [128, DC, E], BF16)
            logits_sb = sbb("logits_sb", [8, T], F32)
            lgTM = sbb("lgTM", [128, NT, 8], F32)
            ctm = sbb("ctm", [128, NT, 8], F32)
            m1 = sbb("m1", [128, NT], F32)
            m2 = sbb("m2", [128, NT], F32)
            t8a = sbb("t8a", [128, 8], F32)
            t8b = sbb("t8b", [128, 8], F32)
            tmask = sbb("tmask", [128, NT, 8], F32)
            t8ball = sbb("t8ball", [128, NT, 8], F32)
            wgB = [sbb(f"wgB{p}", [128, DC, 512], BF16) for p in range(2)]
            wuB = [sbb(f"wuB{p}", [128, DC, 512], BF16) for p in range(2)]
            sg_sb = sbb("sg_sb", [128, T], F32)
            us_sb = sbb("us_sb", [128, T], F32)
            h1_sb = sbb("h1_sb", [128, T], F32)
            hw_bf = sbb("hw_bf", [128, T], BF16)
            cbc_sb = sbb("cbc_sb", [128, T], F32)

            BAR = (("pe", A_END["pe"]), ("act", A_END["act"]), ("dve", A_END["dve"]),
                   ("sc", A_END["sc"]))

            @on("gpsimd", *BAR)
            def _(eng):
                eng.dma_start(out=rwT_sb[:], in_=rwT).then_inc(sems["gpc"], 16)
            RWT_LD = inc("gpc", 16)

            lg_gate = {0: BAR[1], 1: BAR[1]}
            for q in range(NQ):
                bank = psD if q % 2 == 0 else psE

                @on("tensor", ("gpc", RWT_LD), ("trx", LN_TRX), lg_gate[q % 2], *BAR)
                def _(eng, q=q, bank=bank):
                    for d in range(DC):
                        mm = eng.matmul(bank[0:8, :], rwT_sb[:, d, :],
                                        xn_fm[:, d, q * 512:(q + 1) * 512],
                                        start=(d == 0), stop=(d == DC - 1))
                    mm.then_inc(sems["pe"], 1)
                lg_done = inc("pe")

                @on("scalar", ("pe", lg_done))
                def _(eng, q=q, bank=bank):
                    eng.copy(out=logits_sb[:, q * 512:(q + 1) * 512], in_=bank[0:8, :]
                             ).then_inc(sems["act"], 1)
                lg_gate[q % 2] = ("act", inc("act"))
            LGEV = val("act")

            for g4 in range(4):
                @on("tensor", ("act", LGEV), ("dve", val("dve")))
                def _(eng, g4=g4):
                    for j in range(4):
                        tt = g4 * 4 + j
                        mm = eng.matmul(psB[:, j * 8:(j + 1) * 8],
                                        logits_sb[0:8, tt * 128:(tt + 1) * 128],
                                        c_identf[0:8, 0:8], start=True, stop=True)
                    mm.then_inc(sems["pe"], 1)
                trj = inc("pe")

                @on("vector", ("pe", trj))
                def _(eng, g4=g4):
                    eng.tensor_copy(out=lgTM[:, g4 * 4:(g4 + 1) * 4, :],
                                    in_=psB[:, 0:32].rearrange("p (j e) -> p j e", e=8)
                                    ).then_inc(sems["dve"], 1)
                inc("dve")

            @on("vector", ("dve", val("dve")))
            def _(eng):
                for tt in range(NT - 1):
                    eng.reduce_max(out=m1[:, tt:tt + 1], in_=lgTM[:, tt, :], axis=AX.X)
                eng.reduce_max(out=m1[:, NT - 1:NT], in_=lgTM[:, NT - 1, :],
                               axis=AX.X).then_inc(sems["dve"], 1)
            ph1 = inc("dve")

            @on("vector", ("dve", ph1))
            def _(eng):
                for tt in range(NT):
                    lg = lgTM[:, tt, :]
                    eng.tensor_scalar(out=tmask[:, tt, :], in0=lg, scalar1=m1[:, tt:tt + 1],
                                      scalar2=None, op0=ALU.is_ge)
                for tt in range(NT - 1):
                    eng.scalar_tensor_tensor(out=t8ball[:, tt, :], in0=tmask[:, tt, :],
                                             scalar=c_actc[:, 4:5],
                                             in1=lgTM[:, tt, :], op0=ALU.mult, op1=ALU.add)
                eng.scalar_tensor_tensor(out=t8ball[:, NT - 1, :], in0=tmask[:, NT - 1, :],
                                         scalar=c_actc[:, 4:5],
                                         in1=lgTM[:, NT - 1, :], op0=ALU.mult,
                                         op1=ALU.add).then_inc(sems["dve"], 1)
            ph2a = inc("dve")

            @on("vector", ("dve", ph2a))
            def _(eng):
                for tt in range(NT - 1):
                    eng.reduce_max(out=m2[:, tt:tt + 1], in_=t8ball[:, tt, :], axis=AX.X)
                eng.reduce_max(out=m2[:, NT - 1:NT], in_=t8ball[:, NT - 1, :],
                               axis=AX.X).then_inc(sems["dve"], 1)
            ph2 = inc("dve")

            @on("vector", ("dve", ph2))
            def _(eng):
                for tt in range(NT):
                    lg = lgTM[:, tt, :]
                    eng.tensor_scalar(out=tmask[:, tt, :], in0=lg, scalar1=m2[:, tt:tt + 1],
                                      scalar2=None, op0=ALU.is_ge)
                for tt in range(NT):
                    mm = eng.tensor_mul(out=ctm[:, tt, :], in0=tmask[:, tt, :],
                                        in1=lgTM[:, tt, :])
                mm.then_inc(sems["dve"], 1)
            top2_done = inc("dve")

            cbk_gate = ("dve", top2_done)
            for g4 in range(4):
                @on("tensor", ("dve", top2_done), cbk_gate)
                def _(eng, g4=g4):
                    for j in range(4):
                        tt = g4 * 4 + j
                        mm = eng.matmul(psE[0:8, j * 128:(j + 1) * 128], ctm[:, tt, :],
                                        c_identf[:], start=True, stop=True)
                    mm.then_inc(sems["pe"], 1)
                trj = inc("pe")

                @on("vector", ("pe", trj))
                def _(eng, g4=g4):
                    eng.tensor_copy(out=comb_fm[0:8, g4 * 512:(g4 + 1) * 512],
                                    in_=psE[0:8, :]).then_inc(sems["dve"], 1)
                cbk_gate = ("dve", inc("dve"))
            COMB_DVE = val("dve")

            if dev:
                @on("sync", ("dve", COMB_DVE))
                def _(eng):
                    eng.dma_start(out=dev_combfm, in_=comb_fm[:]).then_inc(sems["outd"], 16)
                    eng.dma_start(out=dev_lg, in_=logits_sb[:]).then_inc(sems["outd"], 16)
                    eng.dma_start(out=dev_lgtm, in_=lgTM[:].rearrange("p a e -> p (a e)")).then_inc(sems["outd"], 16)
                    eng.dma_start(out=dev_ctm, in_=ctm[:].rearrange("p a e -> p (a e)")).then_inc(sems["outd"], 16)
                    eng.dma_start(out=dev_m12[:, 0:NT], in_=m1[:]).then_inc(sems["outd"], 16)
                    eng.dma_start(out=dev_m12[:, NT:2 * NT], in_=m2[:]).then_inc(sems["outd"], 16)
                inc("outd", 96)

            stB_st = {"wb": [NONE, NONE], "psA": ("act", A_END["act"]), "prev_sc": val("sc")}

            def emit_wB(u, first=False):
                e, h2 = u // 2, u % 2
                par = u % 2
                gsrc = wg[e, :, h2 * 512:(h2 + 1) * 512] if e < E else sgw[:, h2 * 512:(h2 + 1) * 512]
                usrc = wu[e, :, h2 * 512:(h2 + 1) * 512] if e < E else suw[:, h2 * 512:(h2 + 1) * 512]
                waits = (stB_st["wb"][par],) + (BAR if first else ())

                @on("gpsimd", *waits)
                def _(eng, gsrc=gsrc, usrc=usrc, par=par):
                    g3 = gsrc.rearrange("(d p) i -> p d i", p=128)
                    u3 = usrc.rearrange("(d p) i -> p d i", p=128)
                    for h in range(2):
                        eng.dma_start(out=wgB[par][:, h * 8:(h + 1) * 8, :],
                                      in_=g3[:, h * 8:(h + 1) * 8, :]).then_inc(sems[f"wb{par}"], 16)
                        eng.dma_start(out=wuB[par][:, h * 8:(h + 1) * 8, :],
                                      in_=u3[:, h * 8:(h + 1) * 8, :]).then_inc(sems[f"wb{par}"], 16)
                inc(f"wb{par}", 64)

            emit_wB(0, first=True)
            emit_wB(1, first=True)

            for u in range(2 * (E + 1)):
                e, h2 = u // 2, u % 2
                par = u % 2
                wb_target = 64 * (u // 2 + 1)

                if e < E and h2 == 0:
                    for q in range(NQ):
                        bank = psB if q % 2 == 0 else psD

                        @on("tensor", ("dve", COMB_DVE), ("act", val("act")))
                        def _(eng, e=e, q=q, bank=bank):
                            eng.matmul(bank[:], c_sel8[:, e * 128:(e + 1) * 128],
                                       comb_fm[0:8, q * 512:(q + 1) * 512],
                                       start=True, stop=True).then_inc(sems["pe"], 1)
                        cb_done = inc("pe")

                        @on("scalar", ("pe", cb_done), ("dve", val("dve")))
                        def _(eng, q=q, bank=bank):
                            eng.copy(out=cbc_sb[:, q * 512:(q + 1) * 512], in_=bank[:]
                                     ).then_inc(sems["act"], 1)
                        inc("act")
                cbc_ready = val("act")

                for ic2 in range(4):
                    ic = h2 * 4 + ic2

                    @on("tensor", (f"wb{par}", wb_target), ("dve", LN_DVE),
                        ("trx", LN_TRX), stB_st["psA"], ("act", cbc_ready))
                    def _(eng, par=par, ic2=ic2):
                        for d in range(DC):
                            st = (d == 0)
                            sp = (d == DC - 1)
                            lhs = wgB[par][:, d, ic2 * 128:(ic2 + 1) * 128]
                            for q in range(NQ):
                                mm = eng.matmul(psA[q][:], lhs,
                                                xn_fm[:, d, q * 512:(q + 1) * 512],
                                                start=st, stop=sp)
                        mm.then_inc(sems["pe"], 1)
                    gmm = inc("pe")

                    @on("scalar", ("pe", gmm))
                    def _(eng, e=e, ic=ic):
                        for q in range(NQ):
                            if e < E:
                                a = eng.activation(out=sg_sb[:, q * 512:(q + 1) * 512], in_=psA[q][:],
                                                   func=AF.Silu)
                            else:
                                a = eng.activation(out=sg_sb[:, q * 512:(q + 1) * 512], in_=psA[q][:],
                                                   func=AF.Silu, bias=c_sgb[:, ic:ic + 1])
                        a.then_inc(sems["act"], 1)
                    sg_done = inc("act")

                    @on("tensor", ("act", sg_done))
                    def _(eng, par=par, ic2=ic2):
                        for d in range(DC):
                            st = (d == 0)
                            sp = (d == DC - 1)
                            lhs = wuB[par][:, d, ic2 * 128:(ic2 + 1) * 128]
                            for q in range(NQ):
                                mm = eng.matmul(psA[q][:], lhs,
                                                xn_fm[:, d, q * 512:(q + 1) * 512],
                                                start=st, stop=sp)
                        mm.then_inc(sems["pe"], 1)
                    umm = inc("pe")

                    if e < E:
                        @on("vector", ("pe", umm), ("sc", stB_st["prev_sc"]))
                        def _(eng):
                            for q in range(NQ):
                                eng.tensor_mul(out=h1_sb[:, q * 512:(q + 1) * 512],
                                               in0=sg_sb[:, q * 512:(q + 1) * 512], in1=psA[q][:])
                            eng.tensor_mul(out=hw_bf[:], in0=h1_sb[:],
                                           in1=cbc_sb[:]).then_inc(sems["dve"], 1)
                    else:
                        @on("scalar", ("pe", umm))
                        def _(eng, ic=ic):
                            for q in range(NQ):
                                a = eng.activation(out=us_sb[:, q * 512:(q + 1) * 512], in_=psA[q][:],
                                                   func=AF.Identity, bias=c_sub[:, ic:ic + 1])
                            a.then_inc(sems["act"], 1)
                        usd = inc("act")

                        @on("vector", ("act", usd), ("sc", stB_st["prev_sc"]))
                        def _(eng):
                            eng.tensor_mul(out=hw_bf[:], in0=sg_sb[:],
                                           in1=us_sb[:]).then_inc(sems["dve"], 1)
                    hw_done = inc("dve")
                    stB_st["psA"] = ("dve", hw_done)

                    @on("scalar", ("dve", hw_done))
                    def _(eng, e=e, ic=ic):
                        eng.dma_start(out=hw_sc[e, ic], in_=hw_bf[:]).then_inc(sems["sc"], 16)
                    stB_st["prev_sc"] = inc("sc", 16)

                if u + 2 < 2 * (E + 1):
                    stB_st["wb"][par] = ("pe", val("pe"))
                    emit_wB(u + 2)

            STAGE["on"] = True
            B_END = dict(pe=val("pe"), act=val("act"), dve=val("dve"), sc=val("sc"))

        stABx.close()

        with ExitStack() as stC:
            STAGE["on"] = "c" in stages

            def sbc(name, shape, dtype):
                return stC.enter_context(nc.sbuf_tensor(name, shape, dtype))

            wqo = sbc("wqo", [128, CCN, 512], BF16)
            wqd = sbc("wqd", [128, 64, 512], BF16)
            wqs = sbc("wqs", [128, 8, 512], BF16)
            cact = [sbc(f"cact{p}", [128, CCN, 128], BF16) for p in range(2)]
            hact = [sbc(f"hact{p}", [128, 72, 128], BF16) for p in range(2)]
            xsl = [sbc(f"xsl{p}", [128, 512], F32) for p in range(2)]
            outb = [sbc(f"outb{p}", [128, 512], F32) for p in range(2)]

            BARB = (("pe", B_END["pe"]), ("act", B_END["act"]), ("dve", B_END["dve"]),
                    ("sc", B_END["sc"]))
            stC_st = {"ca": [NONE, NONE], "xr": [NONE, NONE], "outb": [NONE, NONE],
                      "psA": [("dve", B_END["dve"])] * 4, "wc": BARB}

            for q in range(NQ):
                @on("gpsimd", *stC_st["wc"])
                def _(eng, q=q):
                    w3 = woutT.rearrange("(cc p) d -> p cc d", p=128)
                    for h in range(4):
                        eng.dma_start(out=wqo[:, h * 8:(h + 1) * 8, :],
                                      in_=w3[:, h * 8:(h + 1) * 8, q * 512:(q + 1) * 512]
                                      ).then_inc(sems["wc"], 16)
                    for e2 in range(E):
                        eng.dma_start(out=wqd[:, e2 * 8:(e2 + 1) * 8, :],
                                      in_=wd[e2].rearrange("(ic p) d -> p ic d", p=128)[:, :, q * 512:(q + 1) * 512]
                                      ).then_inc(sems["wc"], 16)
                    eng.dma_start(out=wqs[:],
                                  in_=sdw.rearrange("(ic p) d -> p ic d", p=128)[:, :, q * 512:(q + 1) * 512]
                                  ).then_inc(sems["wc"], 16)
                wc_target = inc("wc", 16 * 13)

                for tt in range(NT):
                    cp = tt % 2
                    bank = psA[tt % 4]
                    extra = BARB if (q == 0 and tt < 2) else ()

                    @on("sync", stC_st["ca"][cp], *extra)
                    def _(eng, tt=tt, cp=cp):
                        c3 = combined_sc[:, :, tt * 128:(tt + 1) * 128].rearrange("cc p t -> p cc t")
                        for g in range(4):
                            eng.dma_start(out=cact[cp][:, g * 8:(g + 1) * 8, :],
                                          in_=c3[:, g * 8:(g + 1) * 8, :]).then_inc(sems[f"ca{cp}"], 16)
                        for e2 in range(E + 1):
                            eng.dma_start(out=hact[cp][:, e2 * 8:(e2 + 1) * 8, :],
                                          in_=hw_sc[e2, :, :, tt * 128:(tt + 1) * 128]
                                          .rearrange("ic p t -> p ic t")).then_inc(sems[f"ca{cp}"], 16)
                    ca_target = inc(f"ca{cp}", 16 * 13)

                    @on("sync", stC_st["xr"][cp], *extra)
                    def _(eng, q=q, tt=tt, cp=cp):
                        eng.dma_start(out=xsl[cp][:],
                                      in_=x[tt * 128:(tt + 1) * 128, q * 512:(q + 1) * 512]
                                      ).then_inc(sems[f"xr{cp}"], 16)
                    xr_target = inc(f"xr{cp}", 16)

                    @on("tensor", (f"ca{cp}", ca_target), ("wc", wc_target),
                        ("ldc", NCONST), stC_st["psA"][tt % 4])
                    def _(eng, q=q, tt=tt, cp=cp, bank=bank):
                        eng.matmul(bank[:], c_ones1[:], c_sdb[0:1, q * 512:(q + 1) * 512],
                                   start=True, stop=False)
                        for cc2 in range(CCN):
                            eng.matmul(bank[:], cact[cp][:, cc2, :], wqo[:, cc2, :],
                                       start=False, stop=False)
                        for s_ in range(64):
                            eng.matmul(bank[:], hact[cp][:, s_, :], wqd[:, s_, :],
                                       start=False, stop=False)
                        for ic in range(8):
                            mm = eng.matmul(bank[:], hact[cp][:, 64 + ic, :], wqs[:, ic, :],
                                            start=False, stop=(ic == 7))
                        mm.then_inc(sems["pe"], 1)
                    mm_done = inc("pe")
                    stC_st["ca"][cp] = ("pe", mm_done)

                    @on("vector", ("pe", mm_done), (f"xr{cp}", xr_target), stC_st["outb"][cp])
                    def _(eng, cp=cp, bank=bank):
                        eng.tensor_add(out=outb[cp][:], in0=bank[:], in1=xsl[cp][:]).then_inc(sems["dve"], 1)
                    ev_done = inc("dve")
                    stC_st["psA"][tt % 4] = ("dve", ev_done)
                    stC_st["xr"][cp] = ("dve", ev_done)

                    @on("scalar", ("dve", ev_done))
                    def _(eng, q=q, tt=tt, cp=cp):
                        eng.dma_start(out=out[tt * 128:(tt + 1) * 128, q * 512:(q + 1) * 512],
                                      in_=outb[cp][:]).then_inc(sems["outd"], 16)
                    stC_st["outb"][cp] = ("outd", inc("outd", 16))

                stC_st["wc"] = (("pe", val("pe")),)

            STAGE["on"] = True
            if dev and "a" in stages:
                @on("sync", ("sc", B_END["sc"]))
                def _(eng):
                    eng.dma_start(out=dev_comb, in_=combined_sc).then_inc(sems["outd"], 16)
                inc("outd", 16)
            if dev and "b" in stages:
                @on("sync", ("sc", B_END["sc"]))
                def _(eng):
                    eng.dma_start(out=dev_hw, in_=hw_sc).then_inc(sems["outd"], 16)
                inc("outd", 16)

            @on("scalar", ("outd", val("outd")))
            def _(eng):
                pass

        with nc.Block() as block:
            def runner(name):
                def run(eng):
                    for w, fn in plan[name]:
                        for s, v in w:
                            eng.wait_ge(sems[s], v)
                        fn(eng)
                return run
            block.sync(runner("sync"))
            block.gpsimd(runner("gpsimd"))
            block.tensor(runner("tensor"))
            block.scalar(runner("scalar"))
            block.vector(runner("vector"))

    return nc


def make_in_maps(inputs):
    f32 = np.float32
    cm = np.ascontiguousarray
    xfull = np.asarray(inputs["L0_attn_hidden_states"], f32)
    in_w = np.asarray(inputs["L0_attn_in_proj_w"], f32)
    gate_w = np.asarray(inputs["L0_attn_gate_proj_w"], f32)
    out_w = np.asarray(inputs["L0_attn_out_proj_w"], f32)
    conv_w = np.asarray(inputs["L0_attn_conv_weight"], f32)
    conv_b = np.asarray(inputs["L0_attn_conv_bias"], f32)
    dt_w = np.asarray(inputs["L0_attn_dt_w"], f32)
    dt_b = np.asarray(inputs["L0_attn_dt_b"], f32)
    A_log = np.asarray(inputs["L0_attn_A_log"], f32)
    Bp = np.asarray(inputs["L0_attn_B"], f32)
    Cp = np.asarray(inputs["L0_attn_C"], f32)
    Dp = np.asarray(inputs["L0_attn_D"], f32)
    router_w = np.asarray(inputs["L0_ffn_router_w"], f32)
    wg = cm(np.asarray(inputs["L0_ffn_wg"], f32))
    wu = cm(np.asarray(inputs["L0_ffn_wu"], f32))
    wd = cm(np.asarray(inputs["L0_ffn_wd"], f32))
    sg_w = cm(np.asarray(inputs["L0_ffn_sg_w"], f32))
    su_w = cm(np.asarray(inputs["L0_ffn_su_w"], f32))
    sd_w = cm(np.asarray(inputs["L0_ffn_sd_w"], f32))
    sg_b = np.asarray(inputs["L0_ffn_sg_b"], f32)
    su_b = np.asarray(inputs["L0_ffn_su_b"], f32)
    sd_b = np.asarray(inputs["L0_ffn_sd_b"], f32)

    shared = dict(
        winT=cm(in_w.T), wgT=cm(gate_w.T), woutT=cm(out_w.T),
        convw=cm(conv_w.reshape(32, 128, 4).transpose(1, 0, 2).reshape(128, 128)),
        convb=cm(conv_b.reshape(32, 128).T),
        avecl=cm(A_log.reshape(32, 128).T),
        bvec=cm(Bp.reshape(32, 128).T),
        cvec=cm(Cp.reshape(32, 128).T),
        dvec=cm(Dp.reshape(32, 128).T),
        dtb=cm(dt_b.reshape(32, 2).T),
        rwT=cm(router_w.T.reshape(16, 128, 8).transpose(1, 0, 2).reshape(128, 128)),
        wg=wg, wu=wu, wd=wd, sgw=sg_w, suw=su_w, sdw=sd_w,
        sgb=cm(sg_b.reshape(8, 128).T), sub=cm(su_b.reshape(8, 128).T),
        sdb=cm(sd_b[None, :]),
        sel2=np.repeat(np.eye(2, dtype=f32), 64, axis=1),
        sel8=np.kron(np.eye(8, dtype=f32), np.ones((1, 128), f32)),
        actc=np.tile(np.array([[1e-5, -0.5, -1.0, 1.0, -1e30, 0.0, 0.0, 0.0]], f32), (128, 1)),
        ones1=np.ones((1, 128), f32),
        identf=np.eye(128, dtype=f32),
        identb=np.eye(128).astype(ml_dtypes.bfloat16),
    )
    dtw_l = np.zeros((128, 64), f32)
    for k in range(32):
        dtw_l[0:64, 2 * k] = dt_w[2 * k]
        dtw_l[64:128, 2 * k + 1] = dt_w[2 * k + 1]
    shared["dtw"] = dtw_l

    in_maps = []
    for k in range(NCORE):
        b, qq = k // 4, k % 4
        xs = cm(xfull[b, qq * T:(qq + 1) * T, :])
        xhal = np.zeros((HALO, D), f32) if qq == 0 else cm(xfull[b, qq * T - HALO:qq * T, :])
        gm = np.zeros((128, 4), f32)
        gm[:, :qq] = 1.0
        m = dict(shared)
        m.update(x=xs, xh=xhal, gmask=gm, gm1m=cm(1.0 - gm))
        in_maps.append(m)
    return in_maps


def kernel(**inputs):
    if "nc" not in _BUILD_CACHE:
        _BUILD_CACHE["nc"] = build(dev=False)
    nc = _BUILD_CACHE["nc"]
    in_maps = make_in_maps(inputs)
    res = run_bass_kernel_spmd(nc, in_maps, core_ids=list(range(NCORE)))
    outv = np.empty((2, 8192, 2048), np.float32)
    for k in range(NCORE):
        b, qq = k // 4, k % 4
        outv[b, qq * T:(qq + 1) * T, :] = res.results[k]["out"]
    return outv


# version 05 (final)
# speedup vs baseline: 1.0177x; 1.0177x over previous
import numpy as np
import ml_dtypes
from contextlib import ExitStack

import concourse.bass as bass
import concourse.mybir as mybir
from concourse.bass_utils import run_bass_kernel_spmd

F32 = mybir.dt.float32
BF16 = mybir.dt.bfloat16
AF = mybir.ActivationFunctionType
ALU = mybir.AluOpType
AX = mybir.AxisListType

NCORE = 8
T = 2048
D = 2048
HALO = 3
DC = 16
INNER = 4096
CCN = 32
E = 8
NT = 16
NQ = 4
TPAD = 2176
EPS = 1e-5
ENGS = ("sync", "gpsimd", "tensor", "scalar", "vector")

_BUILD_CACHE = {}


def build(dev=False, stages="abc", ln_tiles=17, dump_xn=True, ln_start=-1):
    nc = bass.Bass("TRN2", target_bir_lowering=False, debug=False)

    def din(name, shape, dtype=F32):
        return nc.dram_tensor(name, shape, dtype, kind="ExternalInput").ap()

    x = din("x", [T, D])
    xh = din("xh", [HALO, D])
    gmask = din("gmask", [128, 4])
    gm1m = din("gm1m", [128, 4])
    winT = din("winT", [D, INNER])
    wgT = din("wgT", [D, INNER])
    woutT = din("woutT", [INNER, D])
    convw = din("convw", [128, 128])
    convb = din("convb", [128, CCN])
    avecl = din("avecl", [128, CCN])
    bvec = din("bvec", [128, CCN])
    cvec = din("cvec", [128, CCN])
    dvec = din("dvec", [128, CCN])
    dtw = din("dtw", [128, 2 * CCN])
    dtb = din("dtb", [2, CCN])
    sel2 = din("sel2", [2, 128])
    actc = din("actc", [128, 8])
    sel8 = din("sel8", [8, 8 * 128])
    ones1 = din("ones1", [1, 128])
    identf = din("identf", [128, 128])
    identb = din("identb", [128, 128], BF16)
    rwT = din("rwT", [128, DC * E])
    wg = din("wg", [E, D, 1024])
    wu = din("wu", [E, D, 1024])
    wd = din("wd", [E, 1024, D])
    sgw = din("sgw", [D, 1024])
    suw = din("suw", [D, 1024])
    sdw = din("sdw", [1024, D])
    sgb = din("sgb", [128, 8])
    sub = din("sub", [128, 8])
    sdb = din("sdb", [1, D])

    out = nc.dram_tensor("out", [T, D], F32, kind="ExternalOutput").ap()

    combined_sc = nc.dram_tensor("combined_sc", [CCN, 128, T], BF16, kind="Internal").ap()
    hw_sc = nc.dram_tensor("hw_sc", [E + 1, 8, 128, T], BF16, kind="Internal").ap()
    ccin = nc.dram_tensor("ccin", [CCN, 2, 128], F32, kind="Internal").ap()
    ccout = nc.dram_tensor("ccout", [CCN, 8, 128], F32, kind="Internal").ap()

    if dev:
        dev_xnfm = nc.dram_tensor("dev_xnfm", [128, DC, TPAD], BF16, kind="ExternalOutput").ap()
        dev_comb = nc.dram_tensor("dev_comb", [CCN, 128, T], BF16, kind="ExternalOutput").ap()
        dev_hw = nc.dram_tensor("dev_hw", [E + 1, 8, 128, T], BF16, kind="ExternalOutput").ap()
        dev_combfm = nc.dram_tensor("dev_combfm", [8, T], F32, kind="ExternalOutput").ap()
        dev_lg = nc.dram_tensor("dev_lg", [8, T], F32, kind="ExternalOutput").ap()
        dev_lgtm = nc.dram_tensor("dev_lgtm", [128, NT * 8], F32, kind="ExternalOutput").ap()
        dev_ctm = nc.dram_tensor("dev_ctm", [128, NT * 8], F32, kind="ExternalOutput").ap()
        dev_m12 = nc.dram_tensor("dev_m12", [128, 2 * NT], F32, kind="ExternalOutput").ap()

    plan = {e: [] for e in ENGS}
    cnt = {}

    STAGE = {"on": True}

    def inc(name, v=1):
        if not STAGE["on"]:
            return cnt.get(name, 0)
        cnt[name] = cnt.get(name, 0) + v
        return cnt[name]

    def val(name):
        return cnt.get(name, 0)

    sems = {}

    def on(eng, *waits):
        if not STAGE["on"]:
            return lambda fn: None
        w = []
        for (s, v) in waits:
            if v > 0:
                assert v <= cnt.get(s, 0), f"wait {s}>={v} references future inc (have {cnt.get(s,0)})"
                w.append((s, v))

        def deco(fn):
            plan[eng].append((w, fn))
            return None
        return deco

    NONE = ("ldc", 0)

    with ExitStack() as stack:
        def sb(name, shape, dtype):
            return stack.enter_context(nc.sbuf_tensor(name, shape, dtype))

        def psum(name, shape, dtype):
            return stack.enter_context(nc.psum_tensor(name, shape, dtype))

        sem_names = (["ldc", "gpc", "ldx0", "ldx1", "wa0", "wa1", "wb0", "wb1",
                      "wc", "ca0", "ca1", "xr0", "xr1", "sc", "sum", "outd", "trx",
                      "pe", "act", "dve"] + [f"cc{k}" for k in range(CCN)])
        for n in sem_names:
            sems[n] = stack.enter_context(nc.semaphore(n))

        c_gmask = sb("c_gmask", [128, 4], F32)
        c_gm1m = sb("c_gm1m", [128, 4], F32)
        c_convw = sb("c_convw", [128, 128], F32)
        c_convb = sb("c_convb", [128, CCN], F32)
        c_avecl = sb("c_avecl", [128, CCN], F32)
        c_aneg = sb("c_aneg", [128, CCN], F32)
        c_bvec = sb("c_bvec", [128, CCN], F32)
        c_cvec = sb("c_cvec", [128, CCN], F32)
        c_dvec = sb("c_dvec", [128, CCN], F32)
        c_dtw = sb("c_dtw", [128, 2 * CCN], F32)
        c_dtb = sb("c_dtb", [2, CCN], F32)
        c_sel2 = sb("c_sel2", [2, 128], F32)
        c_actc = sb("c_actc", [128, 8], F32)
        c_sel8 = sb("c_sel8", [8, 8 * 128], F32)
        c_ones1 = sb("c_ones1", [1, 128], F32)
        c_identf = sb("c_identf", [128, 128], F32)
        c_identb = sb("c_identb", [128, 128], BF16)
        c_sgb = sb("c_sgb", [128, 8], F32)
        c_sub = sb("c_sub", [128, 8], F32)
        c_sdb = sb("c_sdb", [1, D], F32)

        psA = [psum(f"psA{i}", [128, 512], F32) for i in range(4)]
        psB = psum("psB", [128, 512], F32)
        psD = psum("psD", [128, 512], F32)
        psE = psum("psE", [128, 512], F32)

        @on("sync")
        def _(eng):
            for dst, src in ((c_gmask, gmask), (c_gm1m, gm1m), (c_convw, convw),
                             (c_convb, convb), (c_avecl, avecl), (c_bvec, bvec),
                             (c_cvec, cvec), (c_dvec, dvec), (c_dtw, dtw),
                             (c_dtb, dtb), (c_sel2, sel2), (c_sel8, sel8), (c_actc, actc), (c_ones1, ones1),
                             (c_identf, identf), (c_identb, identb),
                             (c_sgb, sgb), (c_sub, sub), (c_sdb, sdb)):
                eng.dma_start(out=dst[:], in_=src).then_inc(sems["ldc"], 16)
        inc("ldc", 16 * 19)
        NCONST = val("ldc")

        @on("scalar", ("ldc", NCONST))
        def _(eng):
            eng.activation(out=c_aneg[:], in_=c_avecl[:], func=AF.Exp).then_inc(sems["act"], 1)
        inc("act")

        @on("vector", ("act", val("act")))
        def _(eng):
            eng.tensor_scalar_mul(c_aneg[:], c_aneg[:], -1.0).then_inc(sems["dve"], 1)
        ANEG_DVE = inc("dve")

        stABx = stack.enter_context(ExitStack())
        xn_fm = stABx.enter_context(nc.sbuf_tensor("xn_fm", [128, DC, TPAD], BF16))

        with ExitStack() as stA:
            stLN = ExitStack()

            def sbl(name, shape, dtype):
                return stLN.enter_context(nc.sbuf_tensor(name, shape, dtype))

            xtile = [sbl(f"xtile{p}", [128, D], F32) for p in range(2)]
            xsq = sbl("xsq", [128, D], F32)
            xn_tm = [sbl(f"xn_tm{p}", [128, D], BF16) for p in range(2)]
            nsum = sbl("nsum", [128, 1], F32)
            sqsum = sbl("sqsum", [128, 1], F32)
            mneg = sbl("mneg", [128, 1], F32)
            ex2 = sbl("ex2", [128, 1], F32)
            negvar = sbl("negvar", [128, 1], F32)
            lnv = sbl("lnv", [128, 1], F32)
            rstd = sbl("rstd", [128, 1], F32)
            lnb = sbl("lnb", [128, 1], F32)

            st_ln = {"xtile": [NONE, NONE], "xntm": [NONE, NONE], "psT": NONE}

            def emit_ln_tile(i):
                p = (i + 1) % 2
                if i == -1:
                    @on("vector")
                    def _(eng, p=p):
                        eng.memset(xtile[p][:], 0.0).then_inc(sems["dve"], 1)
                    ms_done = inc("dve")

                    @on("sync", ("dve", ms_done))
                    def _(eng, p=p):
                        eng.dma_start(out=xtile[p][0:HALO, :], in_=xh).then_inc(sems[f"ldx{p}"], 16)
                else:
                    @on("sync", st_ln["xtile"][p])
                    def _(eng, p=p, i=i):
                        eng.dma_start(out=xtile[p][:], in_=x[i * 128:(i + 1) * 128, :]
                                      ).then_inc(sems[f"ldx{p}"], 16)
                ldv = inc(f"ldx{p}", 16)

                @on("vector", (f"ldx{p}", ldv))
                def _(eng, p=p):
                    eng.reduce_sum(out=nsum[:], in_=xtile[p][:], axis=AX.X, negate=True).then_inc(sems["dve"], 1)
                inc("dve")

                @on("scalar", (f"ldx{p}", ldv))
                def _(eng, p=p):
                    eng.activation(out=xsq[:], in_=xtile[p][:], func=AF.Square,
                                   accum_out=sqsum[:]).then_inc(sems["act"], 1)
                sq_done = inc("act")

                @on("vector", ("act", sq_done))
                def _(eng):
                    eng.tensor_scalar_mul(mneg[:], nsum[:], 1.0 / D)
                    eng.tensor_scalar_mul(ex2[:], sqsum[:], 1.0 / D).then_inc(sems["dve"], 1)
                ms_done = inc("dve")

                @on("vector", ("dve", ms_done))
                def _(eng):
                    eng.tensor_scalar(out=negvar[:], in0=mneg[:], scalar1=mneg[:],
                                      scalar2=ex2[:], op0=ALU.mult, op1=ALU.subtract).then_inc(sems["dve"], 1)
                stats_done = inc("dve")

                @on("scalar", ("dve", stats_done))
                def _(eng):
                    eng.activation(out=lnv[:], in_=negvar[:], func=AF.Ln,
                                   scale=c_actc[:, 2:3], bias=c_actc[:, 0:1])
                    eng.activation(out=rstd[:], in_=lnv[:], func=AF.Exp,
                                   scale=c_actc[:, 1:2]).then_inc(sems["act"], 1)
                rstd_done = inc("act")

                @on("vector", ("act", rstd_done))
                def _(eng):
                    eng.tensor_mul(lnb[:], mneg[:], rstd[:]).then_inc(sems["dve"], 1)
                lnb_done = inc("dve")

                @on("scalar", ("dve", lnb_done), st_ln["xntm"][p])
                def _(eng, p=p):
                    eng.activation(out=xn_tm[p][:], in_=xtile[p][:], func=AF.Identity,
                                   scale=rstd[:], bias=lnb[:]).then_inc(sems["act"], 1)
                xn_done = inc("act")
                st_ln["xtile"][p] = ("act", xn_done)

                if i == -1:
                    @on("sync", ("act", xn_done))
                    def _(eng, p=p):
                        eng.dma_start(out=xn_tm[1 - p][:].rearrange("q (dc t) -> q dc t", t=128),
                                      in_=xn_tm[p][:], transpose=True).then_inc(sems["trx"], 16)
                    trv = inc("trx", 16)

                    @on("vector", ("trx", trv))
                    def _(eng, p=p):
                        eng.tensor_copy(
                            out=xn_fm[:, :, T:T + HALO],
                            in_=xn_tm[1 - p][:].rearrange("q (dc t) -> q dc t", t=128)[:, :, 0:HALO]
                        ).then_inc(sems["dve"], 1)
                    halo_cp = inc("dve")
                    st_ln["xntm"][p] = ("dve", halo_cp)
                    st_ln["xntm"][1 - p] = ("dve", halo_cp)
                else:
                    @on("sync", ("act", xn_done), ("dve", val("dve")))
                    def _(eng, p=p, i=i):
                        off = i * 128
                        eng.dma_start(out=xn_fm[:, :, off:off + 128],
                                      in_=xn_tm[p][:], transpose=True).then_inc(sems["trx"], 16)
                    st_ln["xntm"][p] = ("trx", inc("trx", 16))

            for i in range(ln_start, min(NT, ln_start + ln_tiles)):
                emit_ln_tile(i)
            LN_DVE = val("dve")
            LN_ACT = val("act")
            LN_TRX = val("trx")
            if dev and dump_xn:
                @on("sync", ("dve", LN_DVE), ("trx", LN_TRX))
                def _(eng):
                    eng.dma_start(out=dev_xnfm, in_=xn_fm[:]).then_inc(sems["outd"], 16)
                inc("outd", 16)
            stLN.close()

            STAGE["on"] = "a" in stages

            def sba(name, shape, dtype):
                return stA.enter_context(nc.sbuf_tensor(name, shape, dtype))

            wiT_sb = [sba(f"wiT{p}", [128, DC, 128], BF16) for p in range(2)]
            wgT_sb = [sba(f"wgTs{p}", [128, DC, 128], BF16) for p in range(2)]
            proj = sba("proj", [128, T + HALO], F32)
            acc1 = sba("acc1", [128, T], F32)
            acc2 = sba("acc2", [128, T], F32)
            Xc2 = [sba(f"Xc{p}", [128, T], F32) for p in range(2)]
            gatec = sba("gatec", [128, T], F32)
            a_sb = sba("a_sb", [128, T], F32)
            u_sb = sba("u_sb", [128, T], F32)
            s_sb = sba("s_sb", [128, T], F32)
            comb_bf = sba("comb_bf", [128, T], BF16)
            dtpk = sba("dtpk", [66, T], F32)
            dtraw_t = dtpk[0:2, :]
            dtraw = dtpk[0:2, :]
            dtt2 = dtpk[64:66, :]
            dtt1_t = sba("dtt1", [2, T], F32)
            dtt1 = dtt1_t[:]
            dt_sb = sba("dt_sb", [2, T], F32)
            summ = sba("summ", [128, 2], F32)
            gath = sba("gath", [8, 128], F32)
            gathT = sba("gathT", [128, 8], F32)
            sini = [sba(f"sini{p}", [128, 1], F32) for p in range(2)]
            tmpa = sba("tmpa", [128, 1], F32)
            tmpse = sba("tmpse", [128, 1], F32)

            stA_st = {"wa": [NONE, NONE], "psA": NONE, "psB": NONE, "psD": NONE,
                      "psE": NONE, "prev_comb": 0, "prev_sc": 0, "prev_pub": 0,
                      "prev_scan1": 0, "gath_ld": 0}

            def emit_tail(cc):
                @on("tensor", ("sum", stA_st["gath_ld"]), stA_st["psB"])
                def _(eng):
                    eng.matmul(psB[:, 0:8], gath[0:8, :], c_identf[0:8, 0:8],
                               start=True, stop=True).then_inc(sems["pe"], 1)
                gathT_mm = inc("pe")

                @on("vector", ("pe", gathT_mm))
                def _(eng):
                    eng.tensor_copy(out=gathT[:], in_=psB[:, 0:8]).then_inc(sems["dve"], 1)
                stA_st["psB"] = ("dve", inc("dve"))

                cur, nxt = 0, 1

                @on("vector", ("ldc", NCONST), ("sc", stA_st["prev_sc"]))
                def _(eng):
                    eng.memset(sini[0][:], 0.0).then_inc(sems["dve"], 1)
                fold_step = inc("dve")
                for r in range(3):
                    @on("vector", ("dve", fold_step))
                    def _(eng, r=r, cur=cur, nxt=nxt):
                        eng.tensor_scalar(out=tmpa[:], in0=gathT[:, 2 * r:2 * r + 1],
                                          scalar1=c_gmask[:, r:r + 1], scalar2=c_gm1m[:, r:r + 1],
                                          op0=ALU.mult, op1=ALU.add)
                        eng.tensor_scalar(out=tmpse[:], in0=gathT[:, 2 * r + 1:2 * r + 2],
                                          scalar1=c_gmask[:, r:r + 1], scalar2=None, op0=ALU.mult)
                        eng.scalar_tensor_tensor(out=sini[nxt][:], in0=tmpa[:], scalar=sini[cur][:],
                                                 in1=tmpse[:], op0=ALU.mult,
                                                 op1=ALU.add).then_inc(sems["dve"], 1)
                    fold_step = inc("dve")
                    cur, nxt = nxt, cur

                @on("vector", ("dve", fold_step))
                def _(eng, cc=cc, cur=cur):
                    eng.tensor_tensor_scan(out=s_sb[:], data0=a_sb[:], data1=u_sb[:],
                                           initial=sini[cur][:], op0=ALU.mult, op1=ALU.add)
                    eng.tensor_scalar(out=acc1[:], in0=Xc2[cc % 2][:],
                                      scalar1=c_dvec[:, cc:cc + 1],
                                      scalar2=None, op0=ALU.mult)
                    eng.scalar_tensor_tensor(out=a_sb[:], in0=s_sb[:], scalar=c_cvec[:, cc:cc + 1],
                                             in1=acc1[:], op0=ALU.mult, op1=ALU.add)
                    eng.tensor_mul(out=comb_bf[:], in0=a_sb[:],
                                   in1=gatec[:]).then_inc(sems["dve"], 1)
                combined_done = inc("dve")
                stA_st["prev_comb"] = combined_done

                @on("scalar", ("dve", combined_done))
                def _(eng, cc=cc):
                    eng.dma_start(out=combined_sc[cc], in_=comb_bf[:]).then_inc(sems["sc"], 16)
                stA_st["prev_sc"] = inc("sc", 16)

            def emit_wA(cc, extra=()):
                wp = cc % 2

                @on("gpsimd", stA_st["wa"][wp], *extra)
                def _(eng, cc=cc, wp=wp):
                    wi3 = winT.rearrange("(d p) c -> p d c", p=128)
                    wg3 = wgT.rearrange("(d p) c -> p d c", p=128)
                    for h in range(2):
                        eng.dma_start(out=wiT_sb[wp][:, h * 8:(h + 1) * 8, :],
                                      in_=wi3[:, h * 8:(h + 1) * 8, cc * 128:(cc + 1) * 128]
                                      ).then_inc(sems[f"wa{wp}"], 16)
                        eng.dma_start(out=wgT_sb[wp][:, h * 8:(h + 1) * 8, :],
                                      in_=wg3[:, h * 8:(h + 1) * 8, cc * 128:(cc + 1) * 128]
                                      ).then_inc(sems[f"wa{wp}"], 16)
                inc(f"wa{wp}", 64)

            emit_wA(0, extra=(("dve", LN_DVE), ("act", LN_ACT), ("trx", LN_TRX)))
            emit_wA(1)

            for cc in range(CCN):
                wp = cc % 2
                @on("tensor", (f"wa{wp}", 64 * (cc // 2 + 1)), ("dve", LN_DVE),
                    ("trx", LN_TRX), stA_st["psA"], stA_st["psD"])
                def _(eng, cc=cc, wp=wp):
                    for d in range(DC):
                        st = (d == 0)
                        sp = (d == DC - 1)
                        lhs = wiT_sb[wp][:, d, :]
                        for q in range(NQ):
                            eng.matmul(psA[q][:], lhs,
                                       xn_fm[:, d, q * 512:(q + 1) * 512],
                                       start=st, stop=sp)
                        mm = eng.matmul(psD[:, 0:HALO], lhs, xn_fm[:, d, T:T + HALO], start=st, stop=sp)
                    mm.then_inc(sems["pe"], 1)
                inproj_done = inc("pe")

                @on("scalar", ("pe", inproj_done), ("dve", stA_st["prev_scan1"]))
                def _(eng):
                    eng.copy(out=proj[:, 0:HALO], in_=psD[:, 0:HALO])
                    for q in range(NQ - 1):
                        eng.copy(out=proj[:, HALO + q * 512:HALO + (q + 1) * 512], in_=psA[q][:])
                    eng.copy(out=proj[:, HALO + 3 * 512:HALO + 4 * 512],
                             in_=psA[3][:]).then_inc(sems["act"], 1)
                proj_done = inc("act")
                stA_st["psD"] = ("act", proj_done)

                @on("tensor", ("act", proj_done))
                def _(eng, cc=cc, wp=wp):
                    for d in range(DC):
                        st = (d == 0)
                        sp = (d == DC - 1)
                        lhs = wgT_sb[wp][:, d, :]
                        for q in range(NQ):
                            mm = eng.matmul(psA[q][:], lhs,
                                            xn_fm[:, d, q * 512:(q + 1) * 512],
                                            start=st, stop=sp)
                    mm.then_inc(sems["pe"], 1)
                gatemm_done = inc("pe")

                if cc + 2 < CCN:
                    stA_st["wa"][wp] = ("pe", gatemm_done)
                    emit_wA(cc + 2)

                @on("vector", ("act", proj_done))
                def _(eng, cc=cc):
                    eng.tensor_scalar(out=acc1[:], in0=proj[:, 0:T],
                                      scalar1=c_convw[:, cc * 4:cc * 4 + 1],
                                      scalar2=c_convb[:, cc:cc + 1], op0=ALU.mult, op1=ALU.add)
                    eng.scalar_tensor_tensor(out=acc2[:], in0=proj[:, 1:T + 1],
                                             scalar=c_convw[:, cc * 4 + 1:cc * 4 + 2], in1=acc1[:],
                                             op0=ALU.mult, op1=ALU.add)
                    eng.scalar_tensor_tensor(out=acc1[:], in0=proj[:, 2:T + 2],
                                             scalar=c_convw[:, cc * 4 + 2:cc * 4 + 3], in1=acc2[:],
                                             op0=ALU.mult, op1=ALU.add)
                    eng.scalar_tensor_tensor(out=acc2[:], in0=proj[:, 3:T + 3],
                                             scalar=c_convw[:, cc * 4 + 3:cc * 4 + 4], in1=acc1[:],
                                             op0=ALU.mult, op1=ALU.add).then_inc(sems["dve"], 1)
                conv_done = inc("dve")

                @on("scalar", ("dve", conv_done))
                def _(eng, cc=cc):
                    eng.activation(out=Xc2[cc % 2][:], in_=acc2[:],
                                   func=AF.Silu).then_inc(sems["act"], 1)
                X_done = inc("act")

                if cc >= 1:
                    emit_tail(cc - 1)

                @on("scalar", ("pe", gatemm_done), ("dve", stA_st["prev_comb"]))
                def _(eng):
                    for q in range(NQ):
                        a = eng.activation(out=gatec[:, q * 512:(q + 1) * 512], in_=psA[q][:],
                                           func=AF.Sigmoid)
                    a.then_inc(sems["act"], 1)
                gate_done = inc("act")
                stA_st["psA"] = ("act", gate_done)

                for q in range(NQ):
                    bank, key = (psD, "psD") if q % 2 == 0 else (psE, "psE")

                    @on("tensor", ("act", X_done), stA_st[key])
                    def _(eng, cc=cc, q=q, bank=bank):
                        eng.matmul(bank[0:2, :], c_dtw[:, 2 * cc:2 * cc + 2],
                                   Xc2[cc % 2][:, q * 512:(q + 1) * 512], start=True,
                                   stop=True).then_inc(sems["pe"], 1)
                    mm_done = inc("pe")

                    @on("scalar", ("pe", mm_done))
                    def _(eng, q=q, bank=bank, cc=cc):
                        eng.activation(out=dtpk[0:2, q * 512:(q + 1) * 512], in_=bank[0:2, :],
                                       func=AF.Identity, bias=c_dtb[:, cc:cc + 1]).then_inc(sems["act"], 1)
                    stA_st[key] = ("act", inc("act"))

                @on("scalar", ("act", val("act")))
                def _(eng):
                    eng.activation(out=dt_sb[:], in_=dtraw, func=AF.Relu)
                    eng.activation(out=dtt1[:], in_=dtraw, func=AF.Abs).then_inc(sems["act"], 1)
                inc("act")

                @on("scalar", ("act", val("act")))
                def _(eng):
                    eng.activation(out=dtt2, in_=dtt1, func=AF.Exp,
                                   scale=c_actc[0:2, 2:3]).then_inc(sems["act"], 1)
                inc("act")

                @on("scalar", ("act", val("act")))
                def _(eng):
                    eng.activation(out=dtt1, in_=dtt2, func=AF.Ln,
                                   bias=c_actc[0:2, 3:4]).then_inc(sems["act"], 1)
                ln1p_done = inc("act")

                @on("vector", ("act", ln1p_done))
                def _(eng):
                    eng.tensor_add(out=dt_sb[:], in0=dt_sb[:], in1=dtt1).then_inc(sems["dve"], 1)
                dt_done = inc("dve")

                for q in range(NQ):
                    bank, key = (psB, "psB") if q % 2 == 0 else (psE, "psE")

                    @on("tensor", ("dve", dt_done), stA_st[key])
                    def _(eng, q=q, bank=bank):
                        eng.matmul(bank[:], c_sel2[:], dt_sb[:, q * 512:(q + 1) * 512],
                                   start=True, stop=True).then_inc(sems["pe"], 1)
                    bc_done = inc("pe")

                    @on("scalar", ("pe", bc_done), ("dve", ANEG_DVE))
                    def _(eng, q=q, bank=bank, cc=cc):
                        eng.activation(out=a_sb[:, q * 512:(q + 1) * 512], in_=bank[:],
                                       func=AF.Exp, scale=c_aneg[:, cc:cc + 1]).then_inc(sems["act"], 1)
                    aq_done = inc("act")

                    @on("vector", ("act", aq_done))
                    def _(eng, q=q, bank=bank, cc=cc):
                        eng.scalar_tensor_tensor(out=u_sb[:, q * 512:(q + 1) * 512], in0=bank[:],
                                                 scalar=c_bvec[:, cc:cc + 1],
                                                 in1=Xc2[cc % 2][:, q * 512:(q + 1) * 512],
                                                 op0=ALU.mult, op1=ALU.mult).then_inc(sems["dve"], 1)
                    stA_st[key] = ("dve", inc("dve"))

                @on("vector", ("sum", stA_st["prev_pub"]))
                def _(eng):
                    eng.tensor_tensor_scan(out=s_sb[:], data0=a_sb[:], data1=u_sb[:],
                                           initial=0.0, op0=ALU.mult, op1=ALU.add)
                    eng.tensor_tensor_scan(out=proj[:, 0:T], data0=a_sb[:], data1=a_sb[:],
                                           initial=1.0, op0=ALU.mult, op1=ALU.bypass)
                    eng.tensor_copy(out=summ[:, 0:1], in_=proj[:, T - 1:T])
                    eng.tensor_copy(out=summ[:, 1:2],
                                    in_=s_sb[:, T - 1:T]).then_inc(sems["dve"], 1)
                scan1_done = inc("dve")
                stA_st["prev_scan1"] = scan1_done

                @on("gpsimd", ("dve", scan1_done))
                def _(eng, cc=cc):
                    eng.dma_start(out=ccin[cc, 0:1, :].rearrange("a c -> c a"),
                                  in_=summ[:, 0:1]).then_inc(sems["sum"], 16)
                    eng.dma_start(out=ccin[cc, 1:2, :].rearrange("a c -> c a"),
                                  in_=summ[:, 1:2]).then_inc(sems["sum"], 16)
                pub_done = inc("sum", 32)
                stA_st["prev_pub"] = pub_done

                @on("gpsimd", ("sum", pub_done))
                def _(eng, cc=cc):
                    eng.collective_compute(
                        "AllGather", ALU.bypass,
                        replica_groups=[[0, 1, 2, 3], [4, 5, 6, 7]],
                        ins=[ccin[cc]], outs=[ccout[cc]],
                    ).then_inc(sems[f"cc{cc}"])
                inc(f"cc{cc}")

                @on("gpsimd", (f"cc{cc}", 1))
                def _(eng, cc=cc):
                    eng.dma_start(out=gath[:], in_=ccout[cc]).then_inc(sems["sum"], 16)
                stA_st["gath_ld"] = inc("sum", 16)

            if CCN >= 1 and "a" in stages:
                emit_tail(CCN - 1)

            STAGE["on"] = True
            A_END = dict(pe=val("pe"), act=val("act"), dve=val("dve"), sc=val("sc"))

        with ExitStack() as stB:
            STAGE["on"] = "b" in stages

            def sbb(name, shape, dtype):
                return stB.enter_context(nc.sbuf_tensor(name, shape, dtype))

            comb_fm = sbb("comb_fm", [8, T], F32)
            rwT_sb = sbb("rwT_sb", [128, DC, E], BF16)
            logits_sb = sbb("logits_sb", [8, T], F32)
            lgTM = sbb("lgTM", [128, NT, 8], F32)
            ctm = sbb("ctm", [128, NT, 8], F32)
            m1 = sbb("m1", [128, NT], F32)
            m2 = sbb("m2", [128, NT], F32)
            t8a = sbb("t8a", [128, 8], F32)
            t8b = sbb("t8b", [128, 8], F32)
            tmask = sbb("tmask", [128, NT, 8], F32)
            t8ball = sbb("t8ball", [128, NT, 8], F32)
            wgB = [sbb(f"wgB{p}", [128, DC, 512], BF16) for p in range(2)]
            wuB = [sbb(f"wuB{p}", [128, DC, 512], BF16) for p in range(2)]
            sg_sb = sbb("sg_sb", [128, T], F32)
            us_sb = sbb("us_sb", [128, T], F32)
            h1_sb = sbb("h1_sb", [128, T], F32)
            hw_bf = sbb("hw_bf", [128, T], BF16)
            cbc_sb = sbb("cbc_sb", [128, T], F32)

            BAR = (("pe", A_END["pe"]), ("act", A_END["act"]), ("dve", A_END["dve"]),
                   ("sc", A_END["sc"]))

            @on("gpsimd", *BAR)
            def _(eng):
                eng.dma_start(out=rwT_sb[:], in_=rwT).then_inc(sems["gpc"], 16)
            RWT_LD = inc("gpc", 16)

            lg_gate = {0: BAR[1], 1: BAR[1]}
            for q in range(NQ):
                bank = psD if q % 2 == 0 else psE

                @on("tensor", ("gpc", RWT_LD), ("trx", LN_TRX), lg_gate[q % 2], *BAR)
                def _(eng, q=q, bank=bank):
                    for d in range(DC):
                        mm = eng.matmul(bank[0:8, :], rwT_sb[:, d, :],
                                        xn_fm[:, d, q * 512:(q + 1) * 512],
                                        start=(d == 0), stop=(d == DC - 1))
                    mm.then_inc(sems["pe"], 1)
                lg_done = inc("pe")

                @on("scalar", ("pe", lg_done))
                def _(eng, q=q, bank=bank):
                    eng.copy(out=logits_sb[:, q * 512:(q + 1) * 512], in_=bank[0:8, :]
                             ).then_inc(sems["act"], 1)
                lg_gate[q % 2] = ("act", inc("act"))
            LGEV = val("act")

            for g4 in range(4):
                @on("tensor", ("act", LGEV), ("dve", val("dve")))
                def _(eng, g4=g4):
                    for j in range(4):
                        tt = g4 * 4 + j
                        mm = eng.matmul(psB[:, j * 8:(j + 1) * 8],
                                        logits_sb[0:8, tt * 128:(tt + 1) * 128],
                                        c_identf[0:8, 0:8], start=True, stop=True)
                    mm.then_inc(sems["pe"], 1)
                trj = inc("pe")

                @on("vector", ("pe", trj))
                def _(eng, g4=g4):
                    eng.tensor_copy(out=lgTM[:, g4 * 4:(g4 + 1) * 4, :],
                                    in_=psB[:, 0:32].rearrange("p (j e) -> p j e", e=8)
                                    ).then_inc(sems["dve"], 1)
                inc("dve")

            @on("vector", ("dve", val("dve")))
            def _(eng):
                for tt in range(NT - 1):
                    eng.reduce_max(out=m1[:, tt:tt + 1], in_=lgTM[:, tt, :], axis=AX.X)
                eng.reduce_max(out=m1[:, NT - 1:NT], in_=lgTM[:, NT - 1, :],
                               axis=AX.X).then_inc(sems["dve"], 1)
            ph1 = inc("dve")

            @on("vector", ("dve", ph1))
            def _(eng):
                for tt in range(NT):
                    lg = lgTM[:, tt, :]
                    eng.tensor_scalar(out=tmask[:, tt, :], in0=lg, scalar1=m1[:, tt:tt + 1],
                                      scalar2=None, op0=ALU.is_ge)
                for tt in range(NT - 1):
                    eng.scalar_tensor_tensor(out=t8ball[:, tt, :], in0=tmask[:, tt, :],
                                             scalar=c_actc[:, 4:5],
                                             in1=lgTM[:, tt, :], op0=ALU.mult, op1=ALU.add)
                eng.scalar_tensor_tensor(out=t8ball[:, NT - 1, :], in0=tmask[:, NT - 1, :],
                                         scalar=c_actc[:, 4:5],
                                         in1=lgTM[:, NT - 1, :], op0=ALU.mult,
                                         op1=ALU.add).then_inc(sems["dve"], 1)
            ph2a = inc("dve")

            @on("vector", ("dve", ph2a))
            def _(eng):
                for tt in range(NT - 1):
                    eng.reduce_max(out=m2[:, tt:tt + 1], in_=t8ball[:, tt, :], axis=AX.X)
                eng.reduce_max(out=m2[:, NT - 1:NT], in_=t8ball[:, NT - 1, :],
                               axis=AX.X).then_inc(sems["dve"], 1)
            ph2 = inc("dve")

            @on("vector", ("dve", ph2))
            def _(eng):
                for tt in range(NT):
                    lg = lgTM[:, tt, :]
                    eng.tensor_scalar(out=tmask[:, tt, :], in0=lg, scalar1=m2[:, tt:tt + 1],
                                      scalar2=None, op0=ALU.is_ge)
                for tt in range(NT):
                    mm = eng.tensor_mul(out=ctm[:, tt, :], in0=tmask[:, tt, :],
                                        in1=lgTM[:, tt, :])
                mm.then_inc(sems["dve"], 1)
            top2_done = inc("dve")

            cbk_gate = ("dve", top2_done)
            for g4 in range(4):
                @on("tensor", ("dve", top2_done), cbk_gate)
                def _(eng, g4=g4):
                    for j in range(4):
                        tt = g4 * 4 + j
                        mm = eng.matmul(psE[0:8, j * 128:(j + 1) * 128], ctm[:, tt, :],
                                        c_identf[:], start=True, stop=True)
                    mm.then_inc(sems["pe"], 1)
                trj = inc("pe")

                @on("vector", ("pe", trj))
                def _(eng, g4=g4):
                    eng.tensor_copy(out=comb_fm[0:8, g4 * 512:(g4 + 1) * 512],
                                    in_=psE[0:8, :]).then_inc(sems["dve"], 1)
                cbk_gate = ("dve", inc("dve"))
            COMB_DVE = val("dve")

            if dev:
                @on("sync", ("dve", COMB_DVE))
                def _(eng):
                    eng.dma_start(out=dev_combfm, in_=comb_fm[:]).then_inc(sems["outd"], 16)
                    eng.dma_start(out=dev_lg, in_=logits_sb[:]).then_inc(sems["outd"], 16)
                    eng.dma_start(out=dev_lgtm, in_=lgTM[:].rearrange("p a e -> p (a e)")).then_inc(sems["outd"], 16)
                    eng.dma_start(out=dev_ctm, in_=ctm[:].rearrange("p a e -> p (a e)")).then_inc(sems["outd"], 16)
                    eng.dma_start(out=dev_m12[:, 0:NT], in_=m1[:]).then_inc(sems["outd"], 16)
                    eng.dma_start(out=dev_m12[:, NT:2 * NT], in_=m2[:]).then_inc(sems["outd"], 16)
                inc("outd", 96)

            stB_st = {"wb": [NONE, NONE], "psA": ("act", A_END["act"]), "prev_sc": val("sc")}

            def emit_wB(u, first=False):
                e, h2 = u // 2, u % 2
                par = u % 2
                gsrc = wg[e, :, h2 * 512:(h2 + 1) * 512] if e < E else sgw[:, h2 * 512:(h2 + 1) * 512]
                usrc = wu[e, :, h2 * 512:(h2 + 1) * 512] if e < E else suw[:, h2 * 512:(h2 + 1) * 512]
                waits = (stB_st["wb"][par],) + (BAR if first else ())

                @on("gpsimd", *waits)
                def _(eng, gsrc=gsrc, usrc=usrc, par=par):
                    g3 = gsrc.rearrange("(d p) i -> p d i", p=128)
                    u3 = usrc.rearrange("(d p) i -> p d i", p=128)
                    for h in range(2):
                        eng.dma_start(out=wgB[par][:, h * 8:(h + 1) * 8, :],
                                      in_=g3[:, h * 8:(h + 1) * 8, :]).then_inc(sems[f"wb{par}"], 16)
                        eng.dma_start(out=wuB[par][:, h * 8:(h + 1) * 8, :],
                                      in_=u3[:, h * 8:(h + 1) * 8, :]).then_inc(sems[f"wb{par}"], 16)
                inc(f"wb{par}", 64)

            emit_wB(0, first=True)
            emit_wB(1, first=True)

            for u in range(2 * (E + 1)):
                e, h2 = u // 2, u % 2
                par = u % 2
                wb_target = 64 * (u // 2 + 1)

                if e < E and h2 == 0:
                    for q in range(NQ):
                        bank = psB if q % 2 == 0 else psD

                        @on("tensor", ("dve", COMB_DVE), ("act", val("act")))
                        def _(eng, e=e, q=q, bank=bank):
                            eng.matmul(bank[:], c_sel8[:, e * 128:(e + 1) * 128],
                                       comb_fm[0:8, q * 512:(q + 1) * 512],
                                       start=True, stop=True).then_inc(sems["pe"], 1)
                        cb_done = inc("pe")

                        @on("scalar", ("pe", cb_done), ("dve", val("dve")))
                        def _(eng, q=q, bank=bank):
                            eng.copy(out=cbc_sb[:, q * 512:(q + 1) * 512], in_=bank[:]
                                     ).then_inc(sems["act"], 1)
                        inc("act")
                cbc_ready = val("act")

                for ic2 in range(4):
                    ic = h2 * 4 + ic2

                    @on("tensor", (f"wb{par}", wb_target), ("dve", LN_DVE),
                        ("trx", LN_TRX), stB_st["psA"], ("act", cbc_ready))
                    def _(eng, par=par, ic2=ic2):
                        for d in range(DC):
                            st = (d == 0)
                            sp = (d == DC - 1)
                            lhs = wgB[par][:, d, ic2 * 128:(ic2 + 1) * 128]
                            for q in range(NQ):
                                mm = eng.matmul(psA[q][:], lhs,
                                                xn_fm[:, d, q * 512:(q + 1) * 512],
                                                start=st, stop=sp)
                        mm.then_inc(sems["pe"], 1)
                    gmm = inc("pe")

                    @on("scalar", ("pe", gmm))
                    def _(eng, e=e, ic=ic):
                        for q in range(NQ):
                            if e < E:
                                a = eng.activation(out=sg_sb[:, q * 512:(q + 1) * 512], in_=psA[q][:],
                                                   func=AF.Silu)
                            else:
                                a = eng.activation(out=sg_sb[:, q * 512:(q + 1) * 512], in_=psA[q][:],
                                                   func=AF.Silu, bias=c_sgb[:, ic:ic + 1])
                        a.then_inc(sems["act"], 1)
                    sg_done = inc("act")

                    @on("tensor", ("act", sg_done))
                    def _(eng, par=par, ic2=ic2):
                        for d in range(DC):
                            st = (d == 0)
                            sp = (d == DC - 1)
                            lhs = wuB[par][:, d, ic2 * 128:(ic2 + 1) * 128]
                            for q in range(NQ):
                                mm = eng.matmul(psA[q][:], lhs,
                                                xn_fm[:, d, q * 512:(q + 1) * 512],
                                                start=st, stop=sp)
                        mm.then_inc(sems["pe"], 1)
                    umm = inc("pe")

                    if e < E:
                        @on("vector", ("pe", umm), ("sc", stB_st["prev_sc"]))
                        def _(eng):
                            for q in range(NQ):
                                eng.tensor_mul(out=h1_sb[:, q * 512:(q + 1) * 512],
                                               in0=sg_sb[:, q * 512:(q + 1) * 512], in1=psA[q][:])
                            eng.tensor_mul(out=hw_bf[:], in0=h1_sb[:],
                                           in1=cbc_sb[:]).then_inc(sems["dve"], 1)
                    else:
                        @on("scalar", ("pe", umm))
                        def _(eng, ic=ic):
                            for q in range(NQ):
                                a = eng.activation(out=us_sb[:, q * 512:(q + 1) * 512], in_=psA[q][:],
                                                   func=AF.Identity, bias=c_sub[:, ic:ic + 1])
                            a.then_inc(sems["act"], 1)
                        usd = inc("act")

                        @on("vector", ("act", usd), ("sc", stB_st["prev_sc"]))
                        def _(eng):
                            eng.tensor_mul(out=hw_bf[:], in0=sg_sb[:],
                                           in1=us_sb[:]).then_inc(sems["dve"], 1)
                    hw_done = inc("dve")
                    stB_st["psA"] = ("dve", hw_done)

                    @on("scalar", ("dve", hw_done))
                    def _(eng, e=e, ic=ic):
                        eng.dma_start(out=hw_sc[e, ic], in_=hw_bf[:]).then_inc(sems["sc"], 16)
                    stB_st["prev_sc"] = inc("sc", 16)

                if u + 2 < 2 * (E + 1):
                    stB_st["wb"][par] = ("pe", val("pe"))
                    emit_wB(u + 2)

            STAGE["on"] = True
            B_END = dict(pe=val("pe"), act=val("act"), dve=val("dve"), sc=val("sc"))

        stABx.close()

        with ExitStack() as stC:
            STAGE["on"] = "c" in stages

            def sbc(name, shape, dtype):
                return stC.enter_context(nc.sbuf_tensor(name, shape, dtype))

            wqo = sbc("wqo", [128, CCN, 512], BF16)
            wqd = sbc("wqd", [128, 64, 512], BF16)
            wqs = sbc("wqs", [128, 8, 512], BF16)
            cact = [sbc(f"cact{p}", [128, CCN, 128], BF16) for p in range(2)]
            hact = [sbc(f"hact{p}", [128, 72, 128], BF16) for p in range(2)]
            xsl = [sbc(f"xsl{p}", [128, 512], F32) for p in range(2)]
            outb = [sbc(f"outb{p}", [128, 512], F32) for p in range(2)]

            BARB = (("pe", B_END["pe"]), ("act", B_END["act"]), ("dve", B_END["dve"]),
                    ("sc", B_END["sc"]))
            stC_st = {"ca": [NONE, NONE], "xr": [NONE, NONE], "outb": [NONE, NONE],
                      "psA": [("dve", B_END["dve"])] * 4, "wc": BARB}

            for q in range(NQ):
                @on("gpsimd", *stC_st["wc"])
                def _(eng, q=q):
                    w3 = woutT.rearrange("(cc p) d -> p cc d", p=128)
                    for h in range(4):
                        eng.dma_start(out=wqo[:, h * 8:(h + 1) * 8, :],
                                      in_=w3[:, h * 8:(h + 1) * 8, q * 512:(q + 1) * 512]
                                      ).then_inc(sems["wc"], 16)
                    for e2 in range(E):
                        eng.dma_start(out=wqd[:, e2 * 8:(e2 + 1) * 8, :],
                                      in_=wd[e2].rearrange("(ic p) d -> p ic d", p=128)[:, :, q * 512:(q + 1) * 512]
                                      ).then_inc(sems["wc"], 16)
                    eng.dma_start(out=wqs[:],
                                  in_=sdw.rearrange("(ic p) d -> p ic d", p=128)[:, :, q * 512:(q + 1) * 512]
                                  ).then_inc(sems["wc"], 16)
                wc_target = inc("wc", 16 * 13)

                for tt in range(NT):
                    cp = tt % 2
                    bank = psA[tt % 4]
                    extra = BARB if (q == 0 and tt < 2) else ()

                    @on("sync", stC_st["ca"][cp], *extra)
                    def _(eng, tt=tt, cp=cp):
                        c3 = combined_sc[:, :, tt * 128:(tt + 1) * 128].rearrange("cc p t -> p cc t")
                        for g in range(4):
                            eng.dma_start(out=cact[cp][:, g * 8:(g + 1) * 8, :],
                                          in_=c3[:, g * 8:(g + 1) * 8, :]).then_inc(sems[f"ca{cp}"], 16)
                        for e2 in range(E + 1):
                            eng.dma_start(out=hact[cp][:, e2 * 8:(e2 + 1) * 8, :],
                                          in_=hw_sc[e2, :, :, tt * 128:(tt + 1) * 128]
                                          .rearrange("ic p t -> p ic t")).then_inc(sems[f"ca{cp}"], 16)
                    ca_target = inc(f"ca{cp}", 16 * 13)

                    @on("sync", stC_st["xr"][cp], *extra)
                    def _(eng, q=q, tt=tt, cp=cp):
                        eng.dma_start(out=xsl[cp][:],
                                      in_=x[tt * 128:(tt + 1) * 128, q * 512:(q + 1) * 512]
                                      ).then_inc(sems[f"xr{cp}"], 16)
                    xr_target = inc(f"xr{cp}", 16)

                    @on("tensor", (f"ca{cp}", ca_target), ("wc", wc_target),
                        ("ldc", NCONST), stC_st["psA"][tt % 4])
                    def _(eng, q=q, tt=tt, cp=cp, bank=bank):
                        eng.matmul(bank[:], c_ones1[:], c_sdb[0:1, q * 512:(q + 1) * 512],
                                   start=True, stop=False)
                        for cc2 in range(CCN):
                            eng.matmul(bank[:], cact[cp][:, cc2, :], wqo[:, cc2, :],
                                       start=False, stop=False)
                        for s_ in range(64):
                            eng.matmul(bank[:], hact[cp][:, s_, :], wqd[:, s_, :],
                                       start=False, stop=False)
                        for ic in range(8):
                            mm = eng.matmul(bank[:], hact[cp][:, 64 + ic, :], wqs[:, ic, :],
                                            start=False, stop=(ic == 7))
                        mm.then_inc(sems["pe"], 1)
                    mm_done = inc("pe")
                    stC_st["ca"][cp] = ("pe", mm_done)

                    @on("vector", ("pe", mm_done), (f"xr{cp}", xr_target), stC_st["outb"][cp])
                    def _(eng, cp=cp, bank=bank):
                        eng.tensor_add(out=outb[cp][:], in0=bank[:], in1=xsl[cp][:]).then_inc(sems["dve"], 1)
                    ev_done = inc("dve")
                    stC_st["psA"][tt % 4] = ("dve", ev_done)
                    stC_st["xr"][cp] = ("dve", ev_done)

                    @on("scalar", ("dve", ev_done))
                    def _(eng, q=q, tt=tt, cp=cp):
                        eng.dma_start(out=out[tt * 128:(tt + 1) * 128, q * 512:(q + 1) * 512],
                                      in_=outb[cp][:]).then_inc(sems["outd"], 16)
                    stC_st["outb"][cp] = ("outd", inc("outd", 16))

                stC_st["wc"] = (("pe", val("pe")),)

            STAGE["on"] = True
            if dev and "a" in stages:
                @on("sync", ("sc", B_END["sc"]))
                def _(eng):
                    eng.dma_start(out=dev_comb, in_=combined_sc).then_inc(sems["outd"], 16)
                inc("outd", 16)
            if dev and "b" in stages:
                @on("sync", ("sc", B_END["sc"]))
                def _(eng):
                    eng.dma_start(out=dev_hw, in_=hw_sc).then_inc(sems["outd"], 16)
                inc("outd", 16)

            @on("scalar", ("outd", val("outd")))
            def _(eng):
                pass

        with nc.Block() as block:
            def runner(name):
                def run(eng):
                    for w, fn in plan[name]:
                        for s, v in w:
                            eng.wait_ge(sems[s], v)
                        fn(eng)
                return run
            block.sync(runner("sync"))
            block.gpsimd(runner("gpsimd"))
            block.tensor(runner("tensor"))
            block.scalar(runner("scalar"))
            block.vector(runner("vector"))

    return nc


def make_in_maps(inputs):
    f32 = np.float32
    cm = np.ascontiguousarray
    xfull = np.asarray(inputs["L0_attn_hidden_states"], f32)
    in_w = np.asarray(inputs["L0_attn_in_proj_w"], f32)
    gate_w = np.asarray(inputs["L0_attn_gate_proj_w"], f32)
    out_w = np.asarray(inputs["L0_attn_out_proj_w"], f32)
    conv_w = np.asarray(inputs["L0_attn_conv_weight"], f32)
    conv_b = np.asarray(inputs["L0_attn_conv_bias"], f32)
    dt_w = np.asarray(inputs["L0_attn_dt_w"], f32)
    dt_b = np.asarray(inputs["L0_attn_dt_b"], f32)
    A_log = np.asarray(inputs["L0_attn_A_log"], f32)
    Bp = np.asarray(inputs["L0_attn_B"], f32)
    Cp = np.asarray(inputs["L0_attn_C"], f32)
    Dp = np.asarray(inputs["L0_attn_D"], f32)
    router_w = np.asarray(inputs["L0_ffn_router_w"], f32)
    wg = cm(np.asarray(inputs["L0_ffn_wg"], f32))
    wu = cm(np.asarray(inputs["L0_ffn_wu"], f32))
    wd = cm(np.asarray(inputs["L0_ffn_wd"], f32))
    sg_w = cm(np.asarray(inputs["L0_ffn_sg_w"], f32))
    su_w = cm(np.asarray(inputs["L0_ffn_su_w"], f32))
    sd_w = cm(np.asarray(inputs["L0_ffn_sd_w"], f32))
    sg_b = np.asarray(inputs["L0_ffn_sg_b"], f32)
    su_b = np.asarray(inputs["L0_ffn_su_b"], f32)
    sd_b = np.asarray(inputs["L0_ffn_sd_b"], f32)

    shared = dict(
        winT=cm(in_w.T), wgT=cm(gate_w.T), woutT=cm(out_w.T),
        convw=cm(conv_w.reshape(32, 128, 4).transpose(1, 0, 2).reshape(128, 128)),
        convb=cm(conv_b.reshape(32, 128).T),
        avecl=cm(A_log.reshape(32, 128).T),
        bvec=cm(Bp.reshape(32, 128).T),
        cvec=cm(Cp.reshape(32, 128).T),
        dvec=cm(Dp.reshape(32, 128).T),
        dtb=cm(dt_b.reshape(32, 2).T),
        rwT=cm(router_w.T.reshape(16, 128, 8).transpose(1, 0, 2).reshape(128, 128)),
        wg=wg, wu=wu, wd=wd, sgw=sg_w, suw=su_w, sdw=sd_w,
        sgb=cm(sg_b.reshape(8, 128).T), sub=cm(su_b.reshape(8, 128).T),
        sdb=cm(sd_b[None, :]),
        sel2=np.repeat(np.eye(2, dtype=f32), 64, axis=1),
        sel8=np.kron(np.eye(8, dtype=f32), np.ones((1, 128), f32)),
        actc=np.tile(np.array([[1e-5, -0.5, -1.0, 1.0, -1e30, 0.0, 0.0, 0.0]], f32), (128, 1)),
        ones1=np.ones((1, 128), f32),
        identf=np.eye(128, dtype=f32),
        identb=np.eye(128).astype(ml_dtypes.bfloat16),
    )
    dtw_l = np.zeros((128, 64), f32)
    for k in range(32):
        dtw_l[0:64, 2 * k] = dt_w[2 * k]
        dtw_l[64:128, 2 * k + 1] = dt_w[2 * k + 1]
    shared["dtw"] = dtw_l

    in_maps = []
    for k in range(NCORE):
        b, qq = k // 4, k % 4
        xs = cm(xfull[b, qq * T:(qq + 1) * T, :])
        xhal = np.zeros((HALO, D), f32) if qq == 0 else cm(xfull[b, qq * T - HALO:qq * T, :])
        gm = np.zeros((128, 4), f32)
        gm[:, :qq] = 1.0
        m = dict(shared)
        m.update(x=xs, xh=xhal, gmask=gm, gm1m=cm(1.0 - gm))
        in_maps.append(m)
    return in_maps


def kernel(**inputs):
    if "nc" not in _BUILD_CACHE:
        _BUILD_CACHE["nc"] = build(dev=False)
    nc = _BUILD_CACHE["nc"]
    in_maps = make_in_maps(inputs)
    res = run_bass_kernel_spmd(nc, in_maps, core_ids=list(range(NCORE)))
    outv = np.empty((2, 8192, 2048), np.float32)
    for k in range(NCORE):
        b, qq = k // 4, k % 4
        outv[b, qq * T:(qq + 1) * T, :] = res.results[k]["out"]
    return outv
